# version 20
# baseline (speedup 1.0000x reference)
import numpy as np
import ml_dtypes
import concourse.bass as bass
import concourse.mybir as mybir
from concourse.bass_utils import run_bass_kernel_spmd

F32 = mybir.dt.float32
BF16 = mybir.dt.bfloat16
AF = mybir.ActivationFunctionType
ALU = mybir.AluOpType

D = 2048
ND = 16
T = 2048
TX = 2052
NT = 4
S = 8192
R = 2048
NR = 16
H1 = 512
NH1 = 4
NHEAD = 4
HD = 128
FF = 8192
NF = 64
GROUPS = [[0, 1, 2, 3], [4, 5, 6, 7]]
ISQH = float(1.0 / np.sqrt(128.0))
A_CHUNKS = [(0, 512), (512, 512), (1024, 512), (1536, 512), (2048, 4)]
T_CHUNKS = [(0, 512), (512, 512), (1024, 512), (1536, 512)]

ENGINES = ("sync", "gpsimd", "tensor", "vector", "scalar")


class Builder:

    NROT = 6

    def __init__(self, nc):
        self.nc = nc
        self.prog = {e: [] for e in ENGINES}
        self.cnt = {}
        self.handles = {}
        self._sem_cms = []
        self._rr = {"dS": 0, "dG": 0}
        self._ccn = 0
        names = ["qS", "qG", "qT", "qV", "qA"] + [f"cc{i}" for i in range(8)]
        names += [f"dS{i}" for i in range(self.NROT)]
        names += [f"dG{i}" for i in range(self.NROT)]
        for name in names:
            cm = nc.semaphore(name)
            h = cm.__enter__()
            self._sem_cms.append(cm)
            self.handles[name] = h
            self.cnt[name] = 0

    def _engine_sem(self, eng, dma):
        if dma:
            base = {"sync": "dS", "gpsimd": "dG"}[eng]
            i = self._rr[base]
            self._rr[base] = (i + 1) % self.NROT
            return f"{base}{i}"
        return {"sync": "qS", "gpsimd": "qG", "tensor": "qT", "vector": "qV", "scalar": "qA"}[eng]

    def task(self, eng, deps, emit_fn, dma=False, inc=None, multi=False, use_cc=False):
        if use_cc:
            sem_name = f"cc{self._ccn}"
            self._ccn += 1
        else:
            sem_name = self._engine_sem(eng, dma)
        amount = 16 if dma else 1
        if inc is not None:
            amount = inc
        self.cnt[sem_name] += amount
        target = self.cnt[sem_name]
        sem = self.handles[sem_name]
        deps = [d for d in deps if d is not None]
        handles = self.handles

        def run(e, deps=tuple(deps), emit_fn=emit_fn, sem=sem, amount=amount, multi=multi):
            for dn, dv in deps:
                e.wait_ge(handles[dn], dv)
            if multi:
                emit_fn(e, sem)
            else:
                last = emit_fn(e)
                if last is not None:
                    last.then_inc(sem, amount)

        self.prog[eng].append(run)
        return (sem_name, target)

    def wait(self, eng, deps):
        handles = self.handles

        def run(e, deps=tuple(d for d in deps if d is not None)):
            for dn, dv in deps:
                e.wait_ge(handles[dn], dv)

        self.prog[eng].append(run)

    def barrier(self):
        snapshot = [(n, self.cnt[n]) for n in self.cnt if self.cnt[n] > 0]
        for eng in ("sync", "gpsimd", "tensor", "vector", "scalar"):
            self.task(eng, snapshot, lambda e: e.nop())

    def emit(self):
        nc = self.nc
        with nc.Block() as block:
            @block.sync
            def _(e):
                for f in self.prog["sync"]:
                    f(e)

            @block.gpsimd
            def _(e):
                for f in self.prog["gpsimd"]:
                    f(e)

            @block.tensor
            def _(e):
                for f in self.prog["tensor"]:
                    f(e)

            @block.vector
            def _(e):
                for f in self.prog["vector"]:
                    f(e)

            @block.scalar
            def _(e):
                for f in self.prog["scalar"]:
                    f(e)
        return nc


def build_program(dev=(), phases=("A", "B", "C", "D", "E", "F", "M", "H", "I", "J", "K")):
    dev = set(dev)
    phases = set(phases)
    nc = bass.Bass("TRN2", target_bir_lowering=False)
    b = Builder(nc)
    ctxs = {}

    def dram_in(name, shape, dtype=F32):
        return nc.dram_tensor(name, shape, dtype, kind="ExternalInput").ap()

    def scratch(name, shape, dtype):
        kind = "ExternalOutput" if name in dev else "Internal"
        return nc.dram_tensor(name, shape, dtype, kind=kind).ap()

    def sbuf(name, shape, dtype, side=None):
        cm = nc.sbuf_tensor(name, shape, dtype, side=side) if side else nc.sbuf_tensor(name, shape, dtype)
        t = cm.__enter__()
        ctxs[name] = cm
        return t

    def psum(name, shape, dtype=F32):
        cm = nc.psum_tensor(name, shape, dtype)
        t = cm.__enter__()
        ctxs[name] = cm
        return t

    def free(*names):
        order = list(ctxs.keys())
        for n in sorted(names, key=lambda n: -order.index(n)):
            ctxs.pop(n).__exit__(None, None, None)

    x_ext = dram_in("x_ext", [D, TX])
    wu_t = dram_in("wu_t", [D, R])
    wv_t = dram_in("wv_t", [D, R])
    wg1_t = dram_in("wg1_t", [D, H1])
    wg2_t = dram_in("wg2_t", [H1, R])
    bu_in = dram_in("bu", [128, NR])
    bv_in = dram_in("bv", [128, NR])
    bg1_in = dram_in("bg1", [128, NH1])
    bg2_in = dram_in("bg2", [128, NR])
    convw_in = dram_in("convw", [128, ND, 3])
    convb_in = dram_in("convb", [128, ND])
    ln1w_in = dram_in("ln1w", [128, ND])
    ln2w_in = dram_in("ln2w", [128, ND])
    wq_t = dram_in("wq_t", [D, 512])
    wk_t = dram_in("wk_t", [D, 512])
    wv2_t = dram_in("wv2_t", [D, 512])
    wo_sl = dram_in("wo_sl", [D, 512])
    a_in = dram_in("a_mat", [D, 16])
    b_in = dram_in("b_mat", [16, FF])
    win_t = dram_in("win_t", [D, FF])
    wup_t = dram_in("wup_t", [D, FF])
    wdn_t = dram_in("wdn_t", [FF, D])
    masks_in = dram_in("masks", [128, 4, 512])
    masks2_in = dram_in("masks2", [128, 2, 1024])
    sel_in = dram_in("sel", [4, 1])
    ident_in = dram_in("ident", [128, 128])
    selh_in = dram_in("selh", [64, 16])

    out_ext = nc.dram_tensor("out", [D, T], F32, kind="ExternalOutput").ap()

    xlf_dram = scratch("xlf", [ND, 128, T], F32)
    cs_dram = scratch("cs", [NR, 128, T], BF16)
    out1_dram = scratch("out1", [ND, 128, T], F32)
    h_dram = scratch("h_mlp", [ND, 128, T], BF16)
    gu_dram = scratch("gu", [NF, 128, T], BF16)
    ag1_in = scratch("ag1_in", [D, T], BF16)
    ag1_out = scratch("ag1_out", [4 * D, T], BF16)
    ag2_in = scratch("ag2_in", [128, NR], F32)
    ag2_out = scratch("ag2_out", [4 * 128, NR], F32)
    pref_bounce = scratch("pref_bounce", [1, 128 * NR], F32)
    rs3_in = scratch("rs3_in", [64, T], F32)
    rs3_out = scratch("rs3_out", [16, T], F32)
    modh_dram = scratch("modh", [NHEAD, 16, S], F32)
    wdnbf_dram = scratch("wdnbf", [ND, 128, NF * 128], BF16)
    dbg = {}
    for name, shape, dt in [
        ("y_dbg", [ND, 128, TX], F32), ("h1_dbg", [NH1, 128, T], F32),
        ("ug_dbg", [NR, 128, T], F32), ("tot_dbg", [128, NR], F32),
        ("pref_dbg", [128, NR], F32), ("glob_dbg", [NR, 128, T], F32),
        ("stats_dbg", [4, T], F32), ("xln1_dbg", [ND, 128, T], F32),
        ("mod_dbg", [16, S], F32), ("q_dbg", [NHEAD, 128, S], F32),
        ("vaug_dbg", [NHEAD, 128, 64 * 132], F32), ("o_dbg", [NHEAD, 128, S], F32),
        ("modf_dbg", [16, T], F32), ("ga_dbg", [NF, 128, T], F32),
        ("up_dbg", [NF, 128, T], F32), ("bh_dbg", [NHEAD, 1], F32),
        ("ysb_dbg", [ND, 128, TX], F32), ("msb_dbg", [128, NHEAD, 16], F32),
    ]:
        if name in dev:
            dbg[name] = scratch(name, shape, dt)

    bu_sb = sbuf("bu_sb", [128, NR], F32)
    bv_sb = sbuf("bv_sb", [128, NR], F32)
    bg1_sb = sbuf("bg1_sb", [128, NH1], F32)
    bg2_sb = sbuf("bg2_sb", [128, NR], F32)
    convw_sb = sbuf("convw_sb", [128, ND, 3], F32)
    convb_sb = sbuf("convb_sb", [128, ND], F32)
    ln1w_sb = sbuf("ln1w_sb", [128, ND], F32)
    ln2w_sb = sbuf("ln2w_sb", [128, ND], F32)
    ones_f32 = sbuf("ones_f32", [1, 128], F32)
    ones_bf = sbuf("ones_bf", [1, 128], BF16)
    onesc_bf = sbuf("onesc_bf", [128, 1], BF16)
    onesc_f32 = sbuf("onesc_f32", [128, 1], F32)
    masks2_sb = sbuf("masks2_sb", [128, 2, 1024], BF16)
    sel_sb = sbuf("sel_sb", [4, 1], F32)
    ident_sb = sbuf("ident_sb", [128, 128], BF16)
    selh_sb = sbuf("selh_sb", [64, 16], F32)

    def _pre_dmas(e, sem):
        for dst, src in [(bu_sb, bu_in), (bv_sb, bv_in), (bg1_sb, bg1_in),
                         (bg2_sb, bg2_in), (convb_sb, convb_in), (ln1w_sb, ln1w_in),
                         (convw_sb, convw_in), (ln2w_sb, ln2w_in), (sel_sb, sel_in)]:
            e.dma_start(out=dst[:], in_=src).then_inc(sem, 16)
    state = {}
    t_pre1 = b.task("sync", [], _pre_dmas, dma=True, inc=16 * 9, multi=True)
    t_pre2 = b.task("gpsimd", [], lambda e:
                     e.dma_start(out=masks2_sb[:], in_=masks2_in), dma=True)
    t_pre2b = b.task("gpsimd", [], lambda e:
                     e.dma_start(out=ident_sb[:], in_=ident_in), dma=True)
    t_pre2c = b.task("sync", [], lambda e:
                     e.dma_start(out=selh_sb[:], in_=selh_in), dma=True)
    state["T_IDENT"] = t_pre2b
    state["T_SELH"] = t_pre2c
    t_pre3 = b.task("gpsimd", [], lambda e: [
        e.memset(ones_f32[:], 1.0),
        e.memset(ones_bf[:], 1.0),
        e.memset(onesc_bf[:], 1.0),
        e.memset(onesc_f32[:], 1.0)][-1])
    PRE = [t_pre1, t_pre2, t_pre3]


    if "A" in phases:
        y_sb = sbuf("y_sb", [128, ND, TX], BF16, side="right")
        xbf = y_sb
        xd = [sbuf("xd0", [128, TX], F32), sbuf("xd1", [128, TX], F32)]
        xsq = [sbuf("xsq0", [128, TX], BF16), sbuf("xsq1", [128, TX], BF16)]
        stp = [psum(f"stp{i}", [64, 512], F32) for i in range(5)]
        m1_sb = sbuf("m1_sb", [1, TX], F32)
        r1_sb = sbuf("r1_sb", [1, TX], F32)
        tmp_a = sbuf("tmp_a", [1, TX], F32)
        mbc = sbuf("mbc", [128, TX], F32)
        rbc = sbuf("rbc", [128, TX], F32)
        bcp = psum("bcp", [128, 512], F32)
        yf = [sbuf("yf0", [128, TX], F32), sbuf("yf1", [128, TX], F32)]
        cvs = [sbuf("cvs0", [128, T], F32), sbuf("cvs1", [128, T], F32)]
        cv1 = sbuf("cv1", [128, T], F32)
        xlf_sb = [sbuf("xlf0", [128, T], F32), sbuf("xlf1", [128, T], F32)]

        t_sq = [None] * ND
        t_mmst = [None] * ND
        t_xbf = [None, None]
        for half in range(2):
            t_xbf[half] = b.task("gpsimd", PRE, lambda e, half=half: e.dma_start(
                out=xbf[:, half * 8:(half + 1) * 8, :],
                in_=x_ext[half * 1024:(half + 1) * 1024, :]
                .rearrange("(c p) t -> p c t", p=128)), dma=True)
        for d in range(ND):
            pb = d % 2
            t_sq[d] = b.task("scalar", [t_xbf[d // 8], t_mmst[d - 2] if d >= 2 else None],
                             lambda e, d=d, pb=pb: e.activation(
                                 out=xsq[pb][:], in_=xbf[:, d, :], func=AF.Square))

            def mm_stats(e, d=d, pb=pb):
                last = None
                for i, (c0, w) in enumerate(A_CHUNKS):
                    e.matmul(stp[i][0:1, 0:w], onesc_bf[:, :], xbf[:, d, c0:c0 + w],
                             start=(d == 0), stop=(d == ND - 1))
                    last = e.matmul(stp[i][32:33, 0:w], onesc_bf[:, :], xsq[pb][:, c0:c0 + w],
                                    start=(d == 0), stop=(d == ND - 1))
                return last
            t_mmst[d] = b.task("tensor", [t_sq[d]], mm_stats)

        def stats_math_v(e):
            for i, (c0, w) in enumerate(A_CHUNKS):
                e.tensor_scalar(out=m1_sb[0:1, c0:c0 + w], in0=stp[i][0:1, 0:w],
                                scalar1=1.0 / D, scalar2=None, op0=ALU.mult)
                e.tensor_scalar(out=tmp_a[0:1, c0:c0 + w], in0=stp[i][32:33, 0:w],
                                scalar1=1.0 / D, scalar2=None, op0=ALU.mult)
            e.tensor_tensor(out=r1_sb[0:1, :], in0=m1_sb[0:1, :], in1=m1_sb[0:1, :], op=ALU.mult)
            return e.scalar_tensor_tensor(out=r1_sb[0:1, :], in0=tmp_a[0:1, :], scalar=1e-5,
                                          op0=ALU.add, in1=r1_sb[0:1, :], op1=ALU.subtract)
        t_vstat = b.task("vector", [t_mmst[ND - 1]], stats_math_v)
        t_sstat = b.task("scalar", [t_vstat],
                         lambda e: e.activation(out=tmp_a[0:1, :], in_=r1_sb[0:1, :], func=AF.Sqrt))
        t_rstd = b.task("vector", [t_sstat],
                        lambda e: e.reciprocal(out=r1_sb[0:1, :], in_=tmp_a[0:1, :]))

        t_bc_mm = t_rstd
        t_bc_ev = None
        for j, (src, dst) in enumerate([(m1_sb, mbc), (r1_sb, rbc)]):
            for i, (c0, w) in enumerate(A_CHUNKS):
                t_bc_mm = b.task("tensor", [t_bc_mm, t_bc_ev],
                                 lambda e, src=src, c0=c0, w=w: e.matmul(
                                     bcp[:, 0:w], ones_f32[0:1, :], src[0:1, c0:c0 + w],
                                     start=True, stop=True))
                t_bc_ev = b.task("vector", [t_bc_mm],
                                 lambda e, dst=dst, c0=c0, w=w: e.tensor_copy(
                                     out=dst[:, c0:c0 + w], in_=bcp[:, 0:w]))
        t_bc_done = t_bc_ev

        t_y = [None] * ND
        t_yc = [None] * ND
        t_c0 = [None] * ND
        t_xlf = [None] * ND
        t_st = [None] * ND
        t_ld2 = [None] * ND
        for d in range(ND):
            pb = d % 2
            t_ld2[d] = b.task("sync", [t_bc_done if d < 2 else None,
                                       t_xlf[d - 2] if d >= 2 else None],
                              lambda e, d=d, pb=pb: e.dma_start(
                                  out=xd[pb][:], in_=x_ext[d * 128:(d + 1) * 128, :]),
                              dma=True)

            def em_y(e, d=d, pb=pb):
                e.tensor_tensor(out=yf[pb][:], in0=xbf[:, d, :], in1=mbc[:], op=ALU.subtract)
                return e.tensor_tensor(out=yf[pb][:], in0=yf[pb][:], in1=rbc[:], op=ALU.mult)
            t_y[d] = b.task("vector", [t_bc_done if d < 2 else None,
                                       t_yc[d - 2] if d >= 2 else None], em_y)

            def em_sc(e, d=d, pb=pb):
                e.activation(out=cvs[pb][:], in_=yf[pb][:, 1:2049], func=AF.Identity,
                             bias=convb_sb[:, d:d + 1], scale=convw_sb[:, d, 0:1])
                return e.copy(out=y_sb[:, d, :], in_=yf[pb][:])
            t_c0[d] = b.task("scalar", [t_y[d], t_xlf[d - 2] if d >= 2 else None], em_sc)
            t_yc[d] = t_c0[d]

            def em_conv(e, d=d, pb=pb):
                e.scalar_tensor_tensor(out=cv1[:], in0=yf[pb][:, 2:2050],
                                       scalar=convw_sb[:, d, 1:2], op0=ALU.mult,
                                       in1=cvs[pb][:], op1=ALU.add)
                e.scalar_tensor_tensor(out=cvs[pb][:], in0=yf[pb][:, 3:2051],
                                       scalar=convw_sb[:, d, 2:3], op0=ALU.mult,
                                       in1=cv1[:], op1=ALU.add)
                return e.tensor_tensor(out=xlf_sb[pb][:], in0=xd[pb][:, 2:2050],
                                       in1=cvs[pb][:], op=ALU.add)
            t_xlf[d] = b.task("vector", [t_c0[d], t_ld2[d], t_st[d - 2] if d >= 2 else None], em_conv)
            t_st[d] = b.task("sync", [t_xlf[d]],
                             lambda e, d=d, pb=pb: e.dma_start(out=xlf_dram[d], in_=xlf_sb[pb][:]),
                             dma=True)
            if "y_dbg" in dbg:
                t_ydbg = b.task("gpsimd", [t_c0[d]],
                                lambda e, d=d, pb=pb: e.dma_start(out=dbg["y_dbg"][d], in_=yf[pb][:]),
                                dma=True)
        t_ysbdbg = None
        if "ysb_dbg" in dbg:
            for d in range(ND):
                t_ysbdbg = b.task("gpsimd", [t_yc[ND - 1]], lambda e, d=d: e.dma_start(
                    out=dbg["ysb_dbg"][d], in_=y_sb[:, d, :]), dma=True)
        state["t_y_done"] = [t_yc[ND - 1], t_st[ND - 1]]
        state["A_DONE"] = [t_yc[ND - 1], t_xlf[ND - 1], t_st[ND - 1], t_st[ND - 2], t_ysbdbg]
        state["y_sb"] = y_sb
        state["t_xlf_all"] = t_st
        free("xd0", "xd1", "xsq0", "xsq1", "stp0", "stp1", "stp2", "stp3", "stp4",
             "bcp", "yf0", "yf1", "cvs0", "cvs1", "cv1", "xlf0", "xlf1",
             "mbc", "rbc", "m1_sb", "r1_sb", "tmp_a")

    if "B" in phases:
        dep_y = state["t_y_done"]
        wg2_sb = sbuf("wg2_sb", [128, NH1, R], BF16, side="right")
        h1_sb = sbuf("h1_sb", [128, NH1, T], BF16, side="right")
        totals_sb = sbuf("totals_sb", [128, NR], F32, side="right")
        wg1_sb = sbuf("wg1_sb", [128, ND, 512], BF16)
        phB = [psum("phB0", [128, 512], F32), psum("phB1", [128, 512], F32)]

        A_DONE = state["A_DONE"]
        t_wg1 = b.task("gpsimd", PRE + A_DONE, lambda e: e.dma_start(
            out=wg1_sb[:], in_=wg1_t.rearrange("(c p) h -> p c h", p=128)), dma=True)
        t_wg2 = b.task("gpsimd", PRE + A_DONE, lambda e: e.dma_start(
            out=wg2_sb[:], in_=wg2_t.rearrange("(c p) r -> p c r", p=128)), dma=True)

        t_bev = [None, None]
        t_h1 = []
        for hc in range(NH1):
            for tc in range(NT):
                bank = (hc * NT + tc) % 2

                def em_bmm(e, hc=hc, tc=tc, bank=bank):
                    last = None
                    for d in range(ND):
                        last = e.matmul(phB[bank][:, :], wg1_sb[:, d, hc * 128:(hc + 1) * 128],
                                        y_sb[:, d, 2 + tc * 512:2 + (tc + 1) * 512],
                                        start=(d == 0), stop=(d == ND - 1))
                    return last
                t_bmm = b.task("tensor", dep_y + [t_wg1, t_bev[bank]], em_bmm)
                t_bmm_last = t_bmm
                t_bev[bank] = b.task("scalar", [t_bmm],
                                     lambda e, hc=hc, tc=tc, bank=bank: e.activation(
                                         out=h1_sb[:, hc, tc * 512:(tc + 1) * 512],
                                         in_=phB[bank][:, :], func=AF.Relu,
                                         bias=bg1_sb[:, hc:hc + 1]))
                t_h1.append(t_bev[bank])
        T_H1 = t_h1[-1]
        if "h1_dbg" in dbg:
            for hc in range(NH1):
                b.task("gpsimd", [T_H1], lambda e, hc=hc: e.dma_start(
                    out=dbg["h1_dbg"][hc], in_=h1_sb[:, hc, :]), dma=True)
        free("wg1_sb", "phB0", "phB1")
        state["h1_sb"] = h1_sb
        state["wg2_sb"] = wg2_sb
        state["totals_sb"] = totals_sb
        state["T_H1"] = T_H1
        state["B_MM_DONE"] = t_bmm_last

    if "C" in phases:
        h1_sb = state["h1_sb"]
        wg2_sb = state["wg2_sb"]
        totals_sb = state["totals_sb"]
        T_H1 = state["T_H1"]
        wub = [sbuf("wub0", [128, ND, 128], BF16), sbuf("wub1", [128, ND, 128], BF16)]
        u_sb = sbuf("u_sb", [128, T], F32)
        g_sb = sbuf("g_sb", [128, T], BF16)
        ug_sb = sbuf("ug_sb", [128, T], BF16)
        cso = [sbuf("cso0", [128, T], BF16), sbuf("cso1", [128, T], BF16)]
        puC = [psum("puC0", [128, 512], F32), psum("puC1", [128, 512], F32)]
        pgC = [psum("pgC0", [128, 512], F32), psum("pgC1", [128, 512], F32)]

        t_wuld = [None] * NR
        t_ummN = [None, None]
        t_uevN = [None, None]
        t_gevN = [None, None]
        t_ug = [None] * NR
        t_scan = [None] * NR
        t_csst = [None] * NR
        for r in range(NR):
            pb = r % 2
            t_wuld[r] = b.task("gpsimd", PRE + [t_ummN[pb], state["B_MM_DONE"] if r < 2 else None],
                               lambda e, r=r, pb=pb: e.dma_start(
                                   out=wub[pb][:], in_=wu_t[:, r * 128:(r + 1) * 128]
                                   .rearrange("(c p) r -> p c r", p=128)), dma=True)
            b.task("gpsimd", [], lambda e, r=r: e.dma_start(
                out=wdnbf_dram[r].rearrange("p (f c) -> p f c", f=NF),
                in_=wdn_t[:, r * 128:(r + 1) * 128]
                .rearrange("(f p) c -> p f c", p=128)), dma=True)
            t_uev = [None] * NT
            t_gev = [None] * NT
            for tc in range(NT):
                bank = tc % 2

                def em_umm(e, r=r, pb=pb, tc=tc, bank=bank):
                    last = None
                    for d in range(ND):
                        last = e.matmul(puC[bank][:, :], wub[pb][:, d, :],
                                        y_sb[:, d, 2 + tc * 512:2 + (tc + 1) * 512],
                                        start=(d == 0), stop=(d == ND - 1))
                    return last
                t_umm = b.task("tensor", [t_wuld[r], t_uevN[bank]] +
                               (state["t_y_done"] + [T_H1] if r == 0 else []) +
                               [t_ug[r - 1] if r >= 1 and tc == 0 else None], em_umm)
                t_ummN[pb] = t_umm
                t_uev[tc] = b.task("scalar", [t_umm],
                                   lambda e, r=r, tc=tc, bank=bank: e.activation(
                                       out=u_sb[:, tc * 512:(tc + 1) * 512],
                                       in_=puC[bank][:, :], func=AF.Identity,
                                       bias=bu_sb[:, r:r + 1]))
                t_uevN[bank] = t_uev[tc]

                def em_gmm(e, r=r, tc=tc, bank=bank):
                    last = None
                    for hc in range(NH1):
                        last = e.matmul(pgC[bank][:, :], wg2_sb[:, hc, r * 128:(r + 1) * 128],
                                        h1_sb[:, hc, tc * 512:(tc + 1) * 512],
                                        start=(hc == 0), stop=(hc == NH1 - 1))
                    return last
                t_gmm = b.task("tensor", [T_H1 if r == 0 else None, t_gevN[bank],
                                          t_ug[r - 1] if r >= 1 and tc == 0 else None], em_gmm)
                t_gev[tc] = b.task("scalar", [t_gmm],
                                   lambda e, r=r, tc=tc, bank=bank: e.activation(
                                       out=g_sb[:, tc * 512:(tc + 1) * 512],
                                       in_=pgC[bank][:, :], func=AF.Sigmoid,
                                       bias=bg2_sb[:, r:r + 1]))
                t_gevN[bank] = t_gev[tc]
            t_ug[r] = b.task("vector", [t_uev[NT - 1], t_gev[NT - 1]],
                             lambda e, r=r: e.scalar_tensor_tensor(
                                 out=ug_sb[:], in0=u_sb[:], scalar=1.0, op0=ALU.mult,
                                 in1=g_sb[:], op1=ALU.mult,
                                 accum_out=totals_sb[:, r:r + 1]))
            t_scan[r] = b.task("vector", [t_csst[r - 2] if r >= 2 else None],
                               lambda e, r=r, pb=pb: e.tensor_tensor_scan(
                                   out=cso[pb][:], data0=ug_sb[:], data1=ug_sb[:],
                                   initial=0.0, op0=ALU.add, op1=ALU.bypass))
            t_csst[r] = b.task("sync", [t_scan[r]],
                               lambda e, r=r, pb=pb: e.dma_start(
                                   out=cs_dram[r], in_=cso[pb][:]), dma=True)
            if "ug_dbg" in dbg:
                b.task("gpsimd", [t_scan[r]], lambda e, r=r: e.dma_start(
                    out=dbg["ug_dbg"][r], in_=ug_sb[:]), dma=True)
        T_TOTALS = t_scan[NR - 1]
        T_CS = t_csst[NR - 1]
        if "tot_dbg" in dbg:
            b.task("gpsimd", [T_TOTALS], lambda e: e.dma_start(
                out=dbg["tot_dbg"], in_=totals_sb[:]), dma=True)
        state["T_TOTALS"] = T_TOTALS
        state["T_CS"] = T_CS
        state["t_wu_war"] = t_ummN
        free("wub0", "wub1", "u_sb", "g_sb", "ug_sb", "cso0", "cso1",
             "puC0", "puC1", "pgC0", "pgC1")

    if "D" in phases:
        b.barrier()
        pref_sb = sbuf("pref_sb", [128, NR], F32)
        g2_sb = sbuf("g2_sb", [4, 128, NR], F32)
        prefflat = sbuf("prefflat", [1, 128 * NR], F32)
        prefp = psum("prefp", [1, 512], F32)

        t_ag2w = b.task("sync", [], lambda e: e.dma_start(
            out=ag2_in, in_=state["totals_sb"][:]), dma=True)
        t_ag2 = b.task("gpsimd", [t_ag2w], lambda e: e.collective_compute(
            "AllGather", ALU.bypass, replica_groups=GROUPS,
            ins=[ag2_in.opt()], outs=[ag2_out.opt()]), use_cc=True)
        t_g2ld = b.task("sync", [t_ag2], lambda e: e.dma_start(
            out=g2_sb[:], in_=ag2_out.rearrange("(r p) c -> r p c", p=128)), dma=True)
        t_pev = None
        for i in range(4):
            t_pmm = b.task("tensor", [t_g2ld, t_pev],
                           lambda e, i=i: e.matmul(
                               prefp[:, :], sel_sb[:, :],
                               g2_sb[:, i * 32:(i + 1) * 32, :], start=True, stop=True))
            t_pev = b.task("vector", [t_pmm],
                           lambda e, i=i: e.tensor_copy(
                               out=prefflat[0:1, i * 512:(i + 1) * 512], in_=prefp[:, :]))
        t_pfw = b.task("sync", [t_pev], lambda e: e.dma_start(
            out=pref_bounce, in_=prefflat[0:1, :]), dma=True)
        t_pfr = b.task("sync", [t_pfw], lambda e: e.dma_start(
            out=pref_sb[:], in_=pref_bounce.rearrange("o (p c) -> (o p) c", p=128)), dma=True)
        if "pref_dbg" in dbg:
            b.task("gpsimd", [t_pfr], lambda e: e.dma_start(
                out=dbg["pref_dbg"], in_=pref_sb[:]), dma=True)
        state["T_PREF"] = t_pfr
        free("g2_sb", "prefflat", "prefp")

    if "E" in phases:
        b.barrier()
        h1_sb = state["h1_sb"]
        wg2_sb = state["wg2_sb"]
        glob_sb = sbuf("glob_sb", [128, NR, T], BF16)
        wvb = [sbuf("wvb0", [128, ND, 128], BF16), sbuf("wvb1", [128, ND, 128], BF16)]
        csb = [sbuf("csb0", [128, T], BF16), sbuf("csb1", [128, T], BF16)]
        v_sb = sbuf("v_sb", [128, T], F32)
        gE_sb = sbuf("gE_sb", [128, T], BF16)
        vg_sb = sbuf("vg_sb", [128, T], BF16)
        puE = [psum("puE0", [128, 512], F32), psum("puE1", [128, 512], F32)]
        pgE = [psum("pgE0", [128, 512], F32), psum("pgE1", [128, 512], F32)]

        t_wvld = [None] * NR
        t_csld = [None] * NR
        t_vmmN = [None, None]
        t_vevN = [None, None]
        t_gevN = [None, None]
        t_glob = [None] * NR
        for r in range(NR):
            pb = r % 2
            t_wvld[r] = b.task("gpsimd", [t_vmmN[pb]],
                               lambda e, r=r, pb=pb: e.dma_start(
                                   out=wvb[pb][:], in_=wv_t[:, r * 128:(r + 1) * 128]
                                   .rearrange("(c p) r -> p c r", p=128)), dma=True)
            t_csld[r] = b.task("sync", [t_glob[r - 2] if r >= 2 else None],
                               lambda e, r=r, pb=pb: e.dma_start(
                                   out=csb[pb][:], in_=cs_dram[r]), dma=True)
            t_vev = [None] * NT
            t_gev = [None] * NT
            for tc in range(NT):
                bank = tc % 2

                def em_vmm(e, r=r, pb=pb, tc=tc, bank=bank):
                    last = None
                    for d in range(ND):
                        last = e.matmul(puE[bank][:, :], wvb[pb][:, d, :],
                                        y_sb[:, d, 2 + tc * 512:2 + (tc + 1) * 512],
                                        start=(d == 0), stop=(d == ND - 1))
                    return last
                t_vmm = b.task("tensor", [t_wvld[r], t_vevN[bank],
                                          t_glob[r - 1] if r >= 1 and tc == 0 else None], em_vmm)
                t_vmmN[pb] = t_vmm
                t_vev[tc] = b.task("scalar", [t_vmm],
                                   lambda e, r=r, tc=tc, bank=bank: e.activation(
                                       out=v_sb[:, tc * 512:(tc + 1) * 512],
                                       in_=puE[bank][:, :], func=AF.Identity,
                                       bias=bv_sb[:, r:r + 1]))
                t_vevN[bank] = t_vev[tc]

                def em_gmm(e, r=r, tc=tc, bank=bank):
                    last = None
                    for hc in range(NH1):
                        last = e.matmul(pgE[bank][:, :], wg2_sb[:, hc, r * 128:(r + 1) * 128],
                                        h1_sb[:, hc, tc * 512:(tc + 1) * 512],
                                        start=(hc == 0), stop=(hc == NH1 - 1))
                    return last
                t_gmm = b.task("tensor", [t_gevN[bank],
                                          t_glob[r - 1] if r >= 1 and tc == 0 else None], em_gmm)
                t_gev[tc] = b.task("scalar", [t_gmm],
                                   lambda e, r=r, tc=tc, bank=bank: e.activation(
                                       out=gE_sb[:, tc * 512:(tc + 1) * 512],
                                       in_=pgE[bank][:, :], func=AF.Sigmoid,
                                       bias=bg2_sb[:, r:r + 1]))
                t_gevN[bank] = t_gev[tc]
            t_vg = b.task("vector", [t_vev[NT - 1], t_gev[NT - 1]],
                          lambda e: e.tensor_tensor(
                              out=vg_sb[:], in0=v_sb[:], in1=gE_sb[:], op=ALU.mult))
            t_glob[r] = b.task("vector", [t_csld[r], state["T_PREF"] if r == 0 else None],
                               lambda e, r=r, pb=pb: e.scalar_tensor_tensor(
                                   out=glob_sb[:, r, :], in0=csb[pb][:],
                                   scalar=pref_sb[:, r:r + 1], op0=ALU.add,
                                   in1=vg_sb[:], op1=ALU.mult))
            if "glob_dbg" in dbg:
                b.task("gpsimd", [t_glob[r]], lambda e, r=r: e.dma_start(
                    out=dbg["glob_dbg"][r], in_=glob_sb[:, r, :]), dma=True)
        state["T_GLOB"] = t_glob[NR - 1]
        state["glob_sb"] = glob_sb
        free("wvb0", "wvb1", "csb0", "csb1", "v_sb", "gE_sb", "vg_sb",
             "puE0", "puE1", "pgE0", "pgE1")

    if "F" in phases:
        b.barrier()
        free("totals_sb", "h1_sb", "wg2_sb", "y_sb")
        glob_sb = state["glob_sb"]
        m2_sb = sbuf("m2_sb", [1, T], F32, side="right")
        r2_sb = sbuf("r2_sb", [1, T], F32, side="right")
        a2_sb = sbuf("a2_sb", [1, T], F32, side="right")
        var2_sb = sbuf("var2_sb", [1, T], F32, side="right")
        tmpF_sb = sbuf("tmpF_sb", [1, T], F32, side="right")
        o1 = [sbuf("o1_0", [128, T], F32), sbuf("o1_1", [128, T], F32)]
        wuf = [sbuf("wuf0", [128, NR, 128], BF16), sbuf("wuf1", [128, NR, 128], BF16)]
        xlft = [sbuf("xlft0", [128, T], F32), sbuf("xlft1", [128, T], F32)]
        o1bf = sbuf("o1bf", [128, T], BF16)
        o1sq = sbuf("o1sq", [128, T], BF16)
        pF = [psum("pF0", [128, 512], F32), psum("pF1", [128, 512], F32)]
        stF = [psum(f"stF{i}", [64, 512], F32) for i in range(4)]

        t_wufld = [None] * ND
        t_xlfld = [None] * ND
        t_uvmmN = [None, None]
        t_o1evN = [None, None]
        t_stmm = [None] * ND
        t_o1st = [None] * ND
        t_scA = [None] * ND
        for d in range(ND):
            pb = d % 2
            t_wufld[d] = b.task("gpsimd", [t_uvmmN[pb]],
                                lambda e, d=d, pb=pb: e.dma_start(
                                    out=wuf[pb][:], in_=wu_t[:, d * 128:(d + 1) * 128]
                                    .rearrange("(c p) k -> p c k", p=128)), dma=True)
            t_xlfld[d] = b.task("sync", [t_o1st[d - 2] if d >= 2 else None],
                                lambda e, d=d, pb=pb: e.dma_start(
                                    out=xlft[pb][:], in_=xlf_dram[d]), dma=True)
            t_ev = [None] * NT
            for tc in range(NT):
                bank = tc % 2

                def em_uvmm(e, d=d, pb=pb, tc=tc, bank=bank):
                    last = None
                    for r in range(NR):
                        last = e.matmul(pF[bank][:, :], wuf[pb][:, r, :],
                                        glob_sb[:, r, tc * 512:(tc + 1) * 512],
                                        start=(r == 0), stop=(r == NR - 1))
                    return last
                t_uvmm = b.task("tensor", [t_wufld[d], t_o1evN[bank],
                                           state["T_GLOB"] if d == 0 else None], em_uvmm)
                t_uvmmN[pb] = t_uvmm
                t_ev[tc] = b.task("vector", [t_uvmm, t_xlfld[d],
                                             t_o1st[d - 2] if tc == 0 and d >= 2 else None,
                                             t_scA[d - 2] if tc == 0 and d >= 2 else None],
                                  lambda e, d=d, pb=pb, tc=tc, bank=bank: e.tensor_tensor(
                                      out=o1[pb][:, tc * 512:(tc + 1) * 512],
                                      in0=pF[bank][:, :],
                                      in1=xlft[pb][:, tc * 512:(tc + 1) * 512], op=ALU.add))
                t_o1evN[bank] = t_ev[tc]
            t_o1st[d] = b.task("sync", [t_ev[NT - 1]],
                               lambda e, d=d, pb=pb: e.dma_start(
                                   out=out1_dram[d], in_=o1[pb][:]), dma=True)
            t_sc = b.task("scalar", [t_ev[NT - 1], t_stmm[d - 1] if d >= 1 else None],
                          lambda e, d=d, pb=pb: [
                              e.copy(out=o1bf[:], in_=o1[pb][:]),
                              e.activation(out=o1sq[:], in_=o1[pb][:], func=AF.Square)][-1])
            t_scA[d] = t_sc

            def em_stmm(e, d=d):
                last = None
                for i in range(4):
                    e.matmul(stF[i][0:1, :], onesc_bf[:, :], o1bf[:, i * 512:(i + 1) * 512],
                             start=(d == 0), stop=(d == ND - 1))
                    last = e.matmul(stF[i][32:33, :], onesc_bf[:, :], o1sq[:, i * 512:(i + 1) * 512],
                                    start=(d == 0), stop=(d == ND - 1))
                return last
            t_stmm[d] = b.task("tensor", [t_sc], em_stmm)

        def em_fstat_v1(e):
            for i in range(4):
                e.tensor_scalar(out=m2_sb[0:1, i * 512:(i + 1) * 512], in0=stF[i][0:1, :],
                                scalar1=1.0 / D, scalar2=None, op0=ALU.mult)
                e.tensor_scalar(out=tmpF_sb[0:1, i * 512:(i + 1) * 512], in0=stF[i][32:33, :],
                                scalar1=1.0 / D, scalar2=None, op0=ALU.mult)
            e.tensor_tensor(out=var2_sb[0:1, :], in0=m2_sb[0:1, :], in1=m2_sb[0:1, :], op=ALU.mult)
            e.tensor_tensor(out=var2_sb[0:1, :], in0=tmpF_sb[0:1, :], in1=var2_sb[0:1, :],
                            op=ALU.subtract)
            return e.tensor_scalar(out=tmpF_sb[0:1, :], in0=var2_sb[0:1, :],
                                   scalar1=1e-5, scalar2=None, op0=ALU.add)
        t_fs1 = b.task("vector", [t_stmm[ND - 1]], em_fstat_v1)
        t_fs2 = b.task("scalar", [t_fs1], lambda e: e.activation(
            out=tmpF_sb[0:1, :], in_=tmpF_sb[0:1, :], func=AF.Sqrt))
        t_fs3 = b.task("vector", [t_fs2], lambda e: e.reciprocal(
            out=r2_sb[0:1, :], in_=tmpF_sb[0:1, :]))

        def em_fstat_v2(e):
            e.tensor_tensor(out=tmpF_sb[0:1, :], in0=var2_sb[0:1, :], in1=r2_sb[0:1, :], op=ALU.mult)
            e.tensor_tensor(out=tmpF_sb[0:1, :], in0=tmpF_sb[0:1, :], in1=r2_sb[0:1, :], op=ALU.mult)
            return e.tensor_scalar(out=tmpF_sb[0:1, :], in0=tmpF_sb[0:1, :],
                                   scalar1=1e-6, scalar2=None, op0=ALU.add)
        t_fs4 = b.task("vector", [t_fs3], em_fstat_v2)
        t_fs5 = b.task("scalar", [t_fs4], lambda e: e.activation(
            out=tmpF_sb[0:1, :], in_=tmpF_sb[0:1, :], func=AF.Sqrt))
        t_fs6 = b.task("vector", [t_fs5], lambda e: e.reciprocal(
            out=tmpF_sb[0:1, :], in_=tmpF_sb[0:1, :]))
        t_fs7 = b.task("vector", [t_fs6], lambda e: e.tensor_tensor(
            out=a2_sb[0:1, :], in0=r2_sb[0:1, :], in1=tmpF_sb[0:1, :], op=ALU.mult))
        state["T_STATS2"] = t_fs7
        state["m2_sb"] = m2_sb
        state["r2_sb"] = r2_sb
        if "stats_dbg" in dbg:
            def em_stdbg(e, sem):
                e.dma_start(out=dbg["stats_dbg"][0:1, :], in_=m2_sb[0:1, :]).then_inc(sem, 16)
                e.dma_start(out=dbg["stats_dbg"][1:2, :], in_=var2_sb[0:1, :]).then_inc(sem, 16)
                e.dma_start(out=dbg["stats_dbg"][2:3, :], in_=r2_sb[0:1, :]).then_inc(sem, 16)
                e.dma_start(out=dbg["stats_dbg"][3:4, :], in_=a2_sb[0:1, :]).then_inc(sem, 16)
            b.task("gpsimd", [t_fs7], em_stdbg, dma=True, inc=64, multi=True)

        b.barrier()
        free("wuf0", "wuf1", "xlft0", "xlft1", "o1bf", "o1sq",
             "pF0", "pF1", "stF0", "stF1", "stF2", "stF3")
        mbc2 = sbuf("mbc2", [128, T], F32)
        abc2 = sbuf("abc2", [128, T], F32)
        t1F = sbuf("t1F", [128, T], F32)
        xln1t = sbuf("xln1t", [128, T], BF16)
        ht = sbuf("ht", [128, T], BF16)
        bcp2 = psum("bcp2", [128, 512], F32)

        t_bc_mm = None
        t_bc_ev = None
        for j, (src, dst) in enumerate([(m2_sb, mbc2), (a2_sb, abc2)]):
            for i in range(4):
                t_bc_mm = b.task("tensor", [t_bc_ev],
                                 lambda e, src=src, i=i: e.matmul(
                                     bcp2[:, :], ones_f32[0:1, :], src[0:1, i * 512:(i + 1) * 512],
                                     start=True, stop=True))
                t_bc_ev = b.task("vector", [t_bc_mm],
                                 lambda e, dst=dst, i=i: e.tensor_copy(
                                     out=dst[:, i * 512:(i + 1) * 512], in_=bcp2[:, :]))
        T_BC2 = t_bc_ev

        t_nst = [None] * ND
        t_o1ld = [None] * ND
        t_n1 = [None] * ND
        t_ndbg = [None] * ND
        for d in range(ND):
            pb = d % 2
            t_o1ld[d] = b.task("sync", [t_n1[d - 2] if d >= 2 else None, t_o1st[d]],
                               lambda e, d=d, pb=pb: e.dma_start(
                                   out=o1[pb][:], in_=out1_dram[d]), dma=True)

            def em_n1(e, d=d, pb=pb):
                e.tensor_tensor(out=t1F[:], in0=o1[pb][:], in1=mbc2[:], op=ALU.subtract)
                e.scalar_tensor_tensor(out=xln1t[:], in0=t1F[:], scalar=ln1w_sb[:, d:d + 1],
                                       op0=ALU.mult, in1=abc2[:], op1=ALU.mult)
                return e.scalar_tensor_tensor(out=ht[:], in0=t1F[:], scalar=ln2w_sb[:, d:d + 1],
                                              op0=ALU.mult, in1=abc2[:], op1=ALU.mult)
            t_n1[d] = b.task("vector", [t_o1ld[d], T_BC2 if d < 2 else None,
                                        t_nst[d - 1] if d >= 1 else None,
                                        t_ndbg[d - 1] if d >= 1 else None], em_n1)

            def em_nst(e, sem, d=d):
                e.dma_start(out=ag1_in[d * 128:(d + 1) * 128, :], in_=xln1t[:]).then_inc(sem, 16)
                e.dma_start(out=h_dram[d], in_=ht[:]).then_inc(sem, 16)
            t_nst[d] = b.task("gpsimd", [t_n1[d]], em_nst, dma=True, inc=32, multi=True)
            if "xln1_dbg" in dbg:
                t_ndbg[d] = b.task("gpsimd", [t_n1[d]], lambda e, d=d: e.dma_start(
                    out=dbg["xln1_dbg"][d], in_=xln1t[:]), dma=True)
        state["T_N1"] = t_nst[ND - 1]
        state["T_N1_ALL"] = list(t_nst)
        free("t1F", "xln1t", "ht", "bcp2", "mbc2", "abc2", "o1_0", "o1_1")
        free("glob_sb")
        free("tmpF_sb", "var2_sb")

    if "M" in phases:
        b.barrier()
        msb = sbuf("msb", [128, NHEAD, 16], BF16)
        a_bf = sbuf("a_bf", [128, ND, 16], BF16)
        wo_sb = sbuf("wo_sb", [128, ND, 512], F32)
        a_f32 = sbuf("a_f32", [128, ND, 16], F32)
        pM = psum("pM", [128, 512], F32)

        def em_mld(e, sem):
            e.dma_start(out=wo_sb[:], in_=wo_sl.rearrange("(c p) h -> p c h", p=128)).then_inc(sem, 16)
            e.dma_start(out=a_f32[:], in_=a_in.rearrange("(c p) j -> p c j", p=128)).then_inc(sem, 16)
            e.dma_start(out=a_bf[:], in_=a_in.rearrange("(c p) j -> p c j", p=128)).then_inc(sem, 16)
        t_mld = b.task("gpsimd", [], em_mld, dma=True, inc=48, multi=True)
        t_mev = None
        for ct in range(NHEAD):
            def em_mmm(e, ct=ct):
                last = None
                for d in range(ND):
                    last = e.matmul(pM[:, 0:16], wo_sb[:, d, ct * 128:(ct + 1) * 128],
                                    a_f32[:, d, :], start=(d == 0), stop=(d == ND - 1))
                return last
            t_mmm = b.task("tensor", [t_mld, t_mev], em_mmm)
            t_mev = b.task("scalar", [t_mmm], lambda e, ct=ct: e.copy(
                out=msb[:, ct, :], in_=pM[:, 0:16]))
        state["T_M"] = t_mev
        state["msb"] = msb
        state["a_bf"] = a_bf
        if "msb_dbg" in dbg:
            b.task("gpsimd", [t_mev], lambda e: e.dma_start(
                out=dbg["msb_dbg"], in_=msb[:]), dma=True)
        free("wo_sb", "a_f32", "pM")

    if "H" in phases:
        b.barrier()

        t_ag1 = b.task("gpsimd", state["T_N1_ALL"], lambda e: e.collective_compute(
            "AllGather", ALU.bypass, replica_groups=GROUPS,
            ins=[ag1_in.opt()], outs=[ag1_out.opt()]), use_cc=True)
        wq_sb = [sbuf(f"wq_sb{i}", [128, ND, 128], BF16) for i in range(2)]
        wk_sb = [sbuf(f"wk_sb{i}", [128, ND, 128], BF16) for i in range(2)]
        wv2_sb = [sbuf(f"wv2_sb{i}", [128, ND, 128], BF16) for i in range(2)]
        Q_sb = sbuf("Q_sb", [128, 16, 512], BF16)
        K_sb = sbuf("K_sb", [128, 16, 512], BF16)
        vaug = sbuf("vaug", [128, 64, 129], BF16)
        Gt = [sbuf("Gt0", [128, ND, 512], BF16), sbuf("Gt1", [128, ND, 512], BF16)]
        vtmp = sbuf("vtmp", [128, 512], BF16)
        qsq = sbuf("qsq", [128, 512], BF16)
        PTB = [sbuf(f"PTB{i}", [128, 1024], BF16) for i in range(3)]
        PTrawB = sbuf("PTrawB", [128, 1024], BF16)
        O_sb = [sbuf("O_sb0", [128, 64, 128], BF16), sbuf("O_sb1", [128, 64, 128], BF16)]
        otb = sbuf("otb", [128, 128], BF16)
        rc_sb = sbuf("rc_sb", [128, 1], F32)
        runq = sbuf("runq", [1, 1], F32)
        runk = sbuf("runk", [1, 1], F32)
        nqt = sbuf("nqt", [1, 1], F32)
        bias_h = sbuf("bias_h", [128, 1], F32)
        modev = sbuf("modev", [16, 512], F32)

        def mk_wqkv(h, deps):
            def em_wqkv(e, sem, h=h):
                hp = h % 2
                for dst, src in ((wq_sb, wq_t), (wk_sb, wk_t), (wv2_sb, wv2_t)):
                    e.dma_start(out=dst[hp][:],
                                in_=src[:, h * 128:(h + 1) * 128]
                                .rearrange("(c p) k -> p c k", p=128)).then_inc(sem, 16)
            return b.task("gpsimd", deps, em_wqkv, dma=True, inc=48, multi=True)
        t_wqkv_h = [mk_wqkv(0, [])]

        t_vone = b.task("gpsimd", [], lambda e: e.memset(vaug[:, :, 128:129], 1.0))

        T_MOD_H = []
        t_gld = [None] * 16
        t_chunk_done = None
        t_postpass_done = None
        t_mevG = None
        t_mstG = None
        epi_head = {}

        def post_st(hh, st, ptOp, pmodp, first_dep):
            nonlocal t_mevG, t_mstG
            t_ote = None
            t_mmm = None
            for i in range(4):
                t_otr = b.task("tensor", [epi_head[(hh, st)], first_dep if i == 0 else None,
                                          t_ote],
                               lambda e, hh=hh, st=st, i=i: e.transpose(
                                   ptOp[:, :], O_sb[hh % 2][:, st * 4 + i, :], ident_sb[:, :]))
                t_ote = b.task("vector", [t_otr], lambda e: e.tensor_copy(
                    out=otb[:, :], in_=ptO_dummy_read(ptOp)))
                t_mmm = b.task("tensor", [t_ote, t_mevG if i == 0 else None],
                               lambda e, hh=hh, i=i: e.matmul(
                                   pmodp[:, i * 128:(i + 1) * 128], msb[:, hh, :], otb[:, :],
                                   start=True, stop=True))
            t_mevG = b.task("vector", [t_mmm, t_mstG], lambda e, pmodp=pmodp: e.tensor_copy(
                out=modev[:, :], in_=pmodp[:, :]))
            t_mstG = b.task("gpsimd", [t_mevG], lambda e, hh=hh, st=st: e.dma_start(
                out=modh_dram[hh][:, st * 512:(st + 1) * 512], in_=modev[:, :]), dma=True)
            T_MOD_H.append(t_mstG)

        def ptO_dummy_read(ptOp):
            return ptOp[:, :]

        for h in range(NHEAD):
            pq = psum(f"pq{h}", [128, 512], F32)
            pk = psum(f"pk{h}", [128, 512], F32)
            pv = psum(f"pv{h}", [128, 512], F32)
            ptr2 = [psum(f"ptr{h}a", [128, 128], BF16), psum(f"ptr{h}b", [128, 128], BF16)]
            pnrm = psum(f"pnrm{h}", [64, 512], F32)
            ptOp = psum(f"ptOp{h}", [128, 128], BF16)
            pmodp = psum(f"pmodp{h}", [16, 512], F32)
            t_qev = [None] * 16
            t_kev = [None] * 16
            t_vtr = [None] * 16
            t_nrm = [None] * 16
            for s in range(16):
                pb = s % 2
                c, cc = s // 4, s % 4
                t_gld[s] = b.task("sync",
                                  [t_ag1 if h == 0 else None,
                                   t_chunk_done if h > 0 and s < 2 else None,
                                   t_qev[s - 2] if s >= 2 else None,
                                   t_kev[s - 2] if s >= 2 else None,
                                   t_vtr[s - 2] if s >= 2 else None],
                                  lambda e, s=s, pb=pb, c=c, cc=cc: e.dma_start(
                                      out=Gt[pb][:],
                                      in_=ag1_out[c * D:(c + 1) * D, cc * 512:(cc + 1) * 512]
                                      .rearrange("(d p) t -> p d t", p=128)), dma=True)

                def em_proj(e, h=h, s=s, pb=pb):
                    hp = h % 2
                    for d in range(ND):
                        e.matmul(pq[:, :], wq_sb[hp][:, d, :],
                                 Gt[pb][:, d, :], start=(d == 0), stop=(d == ND - 1))
                    for d in range(ND):
                        e.matmul(pk[:, :], wk_sb[hp][:, d, :],
                                 Gt[pb][:, d, :], start=(d == 0), stop=(d == ND - 1))
                    last = None
                    for d in range(ND):
                        last = e.matmul(pv[:, :], wv2_sb[hp][:, d, :],
                                        Gt[pb][:, d, :], start=(d == 0), stop=(d == ND - 1))
                    return last
                t_proj = b.task("tensor", [t_gld[s], t_wqkv_h[h],
                                           t_qev[s - 1] if s >= 1 else None,
                                           t_kev[s - 1] if s >= 1 else None,
                                           t_vev_last if s >= 1 else None,
                                           t_chunk_done if s == 0 and h > 0 else None], em_proj)
                t_qev[s] = b.task("scalar", [t_proj], lambda e, s=s: e.copy(
                    out=Q_sb[:, s, :], in_=pq[:, :]))
                t_kev[s] = b.task("scalar", [t_proj], lambda e, s=s: e.copy(
                    out=K_sb[:, s, :], in_=pk[:, :]))
                t_vev = b.task("vector", [t_proj], lambda e: e.tensor_copy(
                    out=vtmp[:], in_=pv[:, :]))
                t_vev_last = t_vev
                t_tr2 = [None, None]
                for c4 in range(4):
                    t_trm = b.task("tensor", [t_vev, t_tr2[c4 % 2]],
                                   lambda e, c4=c4: e.transpose(
                                       ptr2[c4 % 2][:, :], vtmp[:, c4 * 128:(c4 + 1) * 128],
                                       ident_sb[:, :]))
                    t_tr2[c4 % 2] = b.task("vector", [t_trm], lambda e, s=s, c4=c4: e.tensor_copy(
                        out=vaug[:, s * 4 + c4, 0:128], in_=ptr2[c4 % 2][:, :]))
                t_vtr[s] = t_tr2[1]
                t_sqq = b.task("scalar", [t_qev[s], t_nrm[s - 1] if s >= 1 else None],
                               lambda e, s=s: e.activation(out=qsq[:], in_=Q_sb[:, s, :],
                                                           func=AF.Square))
                t_nq = b.task("tensor", [t_sqq], lambda e: e.matmul(
                    pnrm[0:1, :], onesc_bf[:, :], qsq[:], start=True, stop=True))
                t_rq = b.task("vector", [t_nq], lambda e, s=s: [
                    e.reduce_max(nqt[0:1, 0:1], pnrm[0:1, :], axis=mybir.AxisListType.X),
                    (e.tensor_tensor(out=runq[0:1, :], in0=runq[0:1, :], in1=nqt[0:1, :], op=ALU.max)
                     if s > 0 else e.tensor_copy(out=runq[0:1, :], in_=nqt[0:1, :]))][-1])
                t_sqk = b.task("scalar", [t_kev[s], t_rq],
                               lambda e, s=s: e.activation(out=qsq[:], in_=K_sb[:, s, :],
                                                           func=AF.Square))
                t_nk = b.task("tensor", [t_sqk], lambda e: e.matmul(
                    pnrm[0:1, :], onesc_bf[:, :], qsq[:], start=True, stop=True))
                t_nrm[s] = b.task("vector", [t_nk], lambda e, s=s: [
                    e.reduce_max(nqt[0:1, 0:1], pnrm[0:1, :], axis=mybir.AxisListType.X),
                    (e.tensor_tensor(out=runk[0:1, :], in0=runk[0:1, :], in1=nqt[0:1, :], op=ALU.max)
                     if s > 0 else e.tensor_copy(out=runk[0:1, :], in_=nqt[0:1, :]))][-1])
                if h >= 1:
                    post_st(h - 1, s, ptOp, pmodp,
                            t_chunk_done if s == 0 else None)
            if h + 1 < NHEAD:
                t_wqkv_h.append(mk_wqkv(h + 1, [t_proj]))
            t_b1 = b.task("vector", [t_nrm[15]], lambda e: e.tensor_tensor(
                out=nqt[0:1, :], in0=runq[0:1, :], in1=runk[0:1, :], op=ALU.mult))
            t_b2 = b.task("scalar", [t_b1], lambda e: e.activation(
                out=nqt[0:1, :], in_=nqt[0:1, :], func=AF.Sqrt, scale=1.0 / 128.0))
            t_b3 = b.task("vector", [t_b2], lambda e: e.tensor_scalar(
                out=nqt[0:1, :], in0=nqt[0:1, :], scalar1=-1.0, scalar2=None, op0=ALU.mult))
            t_b4 = b.task("tensor", [t_b3], lambda e: e.matmul(
                pv[:, 0:1], ones_f32[0:1, :], nqt[0:1, 0:1], start=True, stop=True))
            t_b5 = b.task("vector", [t_b4], lambda e: e.tensor_copy(
                out=bias_h[:, :], in_=pv[:, 0:1]))
            if "bh_dbg" in dbg:
                b.task("gpsimd", [t_b5], lambda e, h=h: e.dma_start(
                    out=dbg["bh_dbg"][h:h + 1, :], in_=nqt[0:1, :]), dma=True)
            free(f"pmodp{h}", f"ptOp{h}", f"pnrm{h}", f"ptr{h}b", f"ptr{h}a", f"pv{h}", f"pk{h}", f"pq{h}")

            psS = [psum(f"psSA{h}", [128, 1024], F32), psum(f"psSB{h}", [128, 1024], F32)]
            psO2 = [psum(f"psO{j}_{h}", [128, 512], F32) for j in range(4)]

            def obank(st, i):
                return psO2[2 * (st % 2) + i // 2][:, (i % 2) * 256:(i % 2) * 256 + 129]

            t_pv = {}
            t_exp = {}
            t_epi_done = {}
            t_sfuse = {}

            def make_pair(st, j):
                pb = j % 2
                ptb = j % 3

                def em_spair(e, h=h, st=st, j=j, pb=pb):
                    last = None
                    for half in range(2):
                        kc = 2 * j + half
                        last = e.matmul(psS[pb][:, half * 512:(half + 1) * 512],
                                        K_sb[:, kc // 4, (kc % 4) * 128:(kc % 4 + 1) * 128],
                                        Q_sb[:, st, :], start=True, stop=True)
                    return last
                if j == 0:
                    t_smm = b.task("tensor", [(t_b5 if st == 0 else None),
                                              (t_postpass_done if st == 0 else None),
                                              t_exp.get((st - 1, 2 * st - 2))
                                              if st >= 1 else None], em_spair)
                else:
                    t_smm = t_sfuse[(st, j)]
                diag = j >= 2 * st
                prev_pv = t_pv.get((st, j - 3))
                if prev_pv is None and st >= 1 and j < 3:
                    cand = [jj for jj in range(2 * st) if jj % 3 == j % 3]
                    if cand:
                        prev_pv = t_pv.get((st - 1, cand[-1]))
                if diag:
                    m = j - 2 * st

                    def em_exp(e, pb=pb):
                        return e.activation(out=PTrawB[:], in_=psS[pb][:, :], func=AF.Exp,
                                            scale=ISQH, bias=bias_h[:, :])
                    t_expr = b.task("scalar", [t_smm,
                                               t_exp.get((st, j - 1)) if j - 1 >= 2 * st
                                               else None], em_exp)
                    t_exp[(st, j)] = b.task("vector", [t_expr, prev_pv],
                                            lambda e, ptb=ptb, m=m: e.tensor_tensor(
                                                out=PTB[ptb][:], in0=PTrawB[:],
                                                in1=masks2_sb[:, m, :], op=ALU.mult))
                else:
                    def em_exp2(e, pb=pb, ptb=ptb):
                        return e.activation(out=PTB[ptb][:], in_=psS[pb][:, :], func=AF.Exp,
                                            scale=ISQH, bias=bias_h[:, :])
                    t_exp[(st, j)] = b.task("scalar", [t_smm, prev_pv], em_exp2)

            def make_fused(st, j):
                npair = 2 * (st + 1)
                ptb = j % 3

                def em_fused(e, st=st, j=j, ptb=ptb, npair=npair):
                    if j + 1 < npair:
                        pb2 = (j + 1) % 2
                        for half in range(2):
                            kc = 2 * (j + 1) + half
                            e.matmul(psS[pb2][:, half * 512:(half + 1) * 512],
                                     K_sb[:, kc // 4, (kc % 4) * 128:(kc % 4 + 1) * 128],
                                     Q_sb[:, st, :], start=True, stop=True)
                    last = None
                    for half in range(2):
                        kc = 2 * j + half
                        for i in range(4):
                            last = e.matmul(obank(st, i),
                                            PTB[ptb][:, half * 512 + i * 128:half * 512 + (i + 1) * 128],
                                            vaug[:, kc, :], start=(kc == 0 and i % 2 == 0),
                                            stop=(j == npair - 1 and half == 1),
                                            skip_group_check=True)
                    return last
                t_pv[(st, j)] = b.task("tensor", [t_exp[(st, j)],
                                                  t_epi_done.get(st - 2) if j == 0 else None,
                                                  t_vone if h == 0 and st == 0 and j == 0
                                                  else None], em_fused)
                t_sfuse[(st, j + 1)] = t_pv[(st, j)]

            def make_epilogue(st):
                npair = 2 * (st + 1)
                t_on = None
                for i in range(4):
                    t_rc = b.task("vector", [t_pv[(st, npair - 1)]],
                                  lambda e, st=st, i=i: e.reciprocal(
                                      out=rc_sb[:, :], in_=obank(st, i)[:, 128:129]))
                    t_on = b.task("vector", [t_rc], lambda e, h=h, st=st, i=i: e.tensor_scalar(
                        out=O_sb[h % 2][:, st * 4 + i, :], in0=obank(st, i)[:, 0:128],
                        scalar1=rc_sb[:, :], scalar2=None, op0=ALU.mult))
                t_epi_done[st] = t_on
                epi_head[(h, st)] = t_on

            for st in range(16):
                npair = 2 * (st + 1)
                make_pair(st, 0)
                for j in range(npair):
                    make_fused(st, j)
                    if j + 1 < npair:
                        make_pair(st, j + 1)
                make_epilogue(st)
            t_chunk_done = t_epi_done[15]
            t_postpass_done = t_mevG
            free(f"psO3_{h}", f"psO2_{h}", f"psO1_{h}", f"psO0_{h}", f"psSB{h}", f"psSA{h}")

        ptOp = psum("ptOp_t", [128, 128], BF16)
        pmodp = psum("pmodp_t", [16, 512], F32)
        for st in range(16):
            post_st(NHEAD - 1, st, ptOp, pmodp, t_chunk_done if st == 0 else None)
        free("pmodp_t", "ptOp_t")
        T_ATTN = T_MOD_H[-1]
        free("modev", "bias_h", "nqt", "runk", "runq", "rc_sb", "otb",
             "PTrawB", "PTB2", "PTB1", "PTB0", "qsq", "vtmp", "Gt1", "Gt0", "vaug",
             "K_sb", "Q_sb", "wv2_sb0", "wv2_sb1", "wk_sb0", "wk_sb1",
             "wq_sb0", "wq_sb1", "O_sb1", "O_sb0")
        state["T_ATTN"] = T_ATTN

    if "I" in phases:
        b.barrier()
        mh_sb = sbuf("mh_sb", [64, 512], F32)
        modev2 = sbuf("modev2", [16, 512], F32)
        modt = sbuf("modt", [16, T], BF16, side="right")
        modt_in = sbuf("modt_in", [16, T], F32)
        pI = psum("pI", [16, 512], F32)
        t_prev_ev = None
        t_ld_prev = None
        t_st_all = []
        for tcS in range(16):
            cg, tl = tcS // 4, tcS % 4
            t_mld = b.task("sync", [t_prev_ev],
                           lambda e, tcS=tcS: e.dma_start(
                               out=mh_sb[:], in_=modh_dram[:, :, tcS * 512:(tcS + 1) * 512]
                               .rearrange("h j t -> (h j) t")), dma=True)
            t_msum = b.task("tensor", [t_mld, t_prev_ev], lambda e: e.matmul(
                pI[:, :], selh_sb[:, :], mh_sb[:], start=True, stop=True))
            t_prev_ev = b.task("vector", [t_msum, t_ld_prev], lambda e: e.tensor_copy(
                out=modev2[:, :], in_=pI[:, :]))
            t_st = b.task("sync", [t_prev_ev], lambda e, cg=cg, tl=tl: e.dma_start(
                out=rs3_in.rearrange("(c j) t -> j c t", j=16)[:, cg, tl * 512:(tl + 1) * 512],
                in_=modev2[:, :]), dma=True)
            if "mod_dbg" in dbg:
                b.task("gpsimd", [t_prev_ev], lambda e, tcS=tcS: e.dma_start(
                    out=dbg["mod_dbg"][:, tcS * 512:(tcS + 1) * 512], in_=modev2[:, :]), dma=True)
            t_ld_prev = t_st
            t_st_all.append(t_st)
        t_rs3 = b.task("gpsimd", t_st_all, lambda e: e.collective_compute(
            "ReduceScatter", ALU.add, replica_groups=GROUPS,
            ins=[rs3_in.opt()], outs=[rs3_out.opt()]), use_cc=True)
        t_mtl = b.task("sync", [t_rs3], lambda e: e.dma_start(
            out=modt_in[:], in_=rs3_out), dma=True)
        t_tanh = b.task("scalar", [t_mtl], lambda e: e.activation(
            out=modt[:, :], in_=modt_in[:, :], func=AF.Tanh))
        if "modf_dbg" in dbg:
            b.task("gpsimd", [t_tanh], lambda e: e.dma_start(
                out=dbg["modf_dbg"], in_=modt[:, :]), dma=True)
        state["T_MODT"] = t_tanh
        state["modt"] = modt
        free("pI", "modt_in", "modev2", "mh_sb")

    if "J" in phases:
        b.barrier()
        modt = state["modt"]
        h_sb = sbuf("h_sb", [128, ND, T], BF16)
        hA_sb = sbuf("hA_sb", [16, T], F32)
        modA = sbuf("modA", [16, T], BF16)
        b_sb = sbuf("b_sb", [16, FF], BF16)
        winb = [sbuf("winb0", [128, ND, 128], BF16), sbuf("winb1", [128, ND, 128], BF16)]
        wupb = [sbuf("wupb0", [128, ND, 128], BF16), sbuf("wupb1", [128, ND, 128], BF16)]
        gateb = sbuf("gateb", [128, NT, 512], BF16)
        upb = sbuf("upb", [128, NT, 512], BF16)
        gub = [sbuf("gub0", [128, T], BF16), sbuf("gub1", [128, T], BF16)]
        pgJ = [psum(f"pgJ{i}", [128, 512], F32) for i in range(4)]
        pqJ = [psum(f"pqJ{i}", [128, 512], F32) for i in range(4)]

        t_hld = b.task("sync", [], lambda e: e.dma_start(
            out=h_sb[:], in_=h_dram.rearrange("d p t -> p d t")), dma=True)
        t_bld = b.task("gpsimd", [], lambda e: e.dma_start(out=b_sb[:], in_=b_in), dma=True)
        t_haev = None
        for tc in range(NT):
            def em_ha(e, tc=tc):
                last = None
                for d in range(ND):
                    last = e.matmul(pgJ[0][0:16, :], state["a_bf"][:, d, :],
                                    h_sb[:, d, tc * 512:(tc + 1) * 512],
                                    start=(d == 0), stop=(d == ND - 1))
                return last
            t_ha = b.task("tensor", [t_hld, state["T_M"], t_haev], em_ha)
            t_haev = b.task("vector", [t_ha], lambda e, tc=tc: e.tensor_copy(
                out=hA_sb[:, tc * 512:(tc + 1) * 512], in_=pgJ[0][0:16, :]))
        t_modA = b.task("vector", [t_haev, state["T_MODT"]], lambda e: e.scalar_tensor_tensor(
            out=modA[:, :], in0=hA_sb[:, :], scalar=0.1, op0=ALU.mult,
            in1=modt[:, :], op1=ALU.mult))

        t_wld = [None] * NF
        t_guv4 = [None] * NT
        t_mmlast = [None, None]
        t_gev4 = [None] * NT
        t_uev4 = [None] * NT
        t_gust = [None] * NF
        for f in range(NF):
            pb = f % 2

            def em_wld(e, sem, f=f, pb=pb):
                e.dma_start(out=winb[pb][:], in_=win_t[:, f * 128:(f + 1) * 128]
                            .rearrange("(c p) k -> p c k", p=128)).then_inc(sem, 16)
                e.dma_start(out=wupb[pb][:], in_=wup_t[:, f * 128:(f + 1) * 128]
                            .rearrange("(c p) k -> p c k", p=128)).then_inc(sem, 16)
            t_wld[f] = b.task("gpsimd", [t_mmlast[pb]], em_wld, dma=True, inc=32, multi=True)

            def em_gmm(e, f=f, pb=pb):
                for d in range(ND):
                    for tc in range(NT):
                        e.matmul(pgJ[tc][:, :], winb[pb][:, d, :],
                                 h_sb[:, d, tc * 512:(tc + 1) * 512],
                                 start=(d == 0), stop=False)
                last = None
                for tc in range(NT):
                    last = e.matmul(pgJ[tc][:, :], b_sb[0:16, f * 128:(f + 1) * 128],
                                    modA[:, tc * 512:(tc + 1) * 512], start=False, stop=True)
                return last
            t_gmm = b.task("tensor", [t_wld[f], t_modA if f == 0 else None] +
                           t_gev4, em_gmm)

            def em_umm(e, f=f, pb=pb):
                last = None
                for d in range(ND):
                    for tc in range(NT):
                        last = e.matmul(pqJ[tc][:, :], wupb[pb][:, d, :],
                                        h_sb[:, d, tc * 512:(tc + 1) * 512],
                                        start=(d == 0), stop=(d == ND - 1))
                return last
            t_umm = b.task("tensor", t_uev4 + [], em_umm)
            t_mmlast[pb] = t_umm
            t_guv = None
            for tc in range(NT):
                t_gev4[tc] = b.task("scalar", [t_gmm, t_guv4[tc]], lambda e, tc=tc: e.activation(
                    out=gateb[:, tc, :], in_=pgJ[tc][:, :], func=AF.Silu))
                t_uev4[tc] = b.task("vector", [t_umm, t_guv4[tc]], lambda e, tc=tc: e.tensor_copy(
                    out=upb[:, tc, :], in_=pqJ[tc][:, :]))
                t_guv = b.task("vector", [t_gev4[tc], t_uev4[tc],
                                          t_gust[f - 2] if tc == 0 and f >= 2 else None],
                               lambda e, f=f, pb=pb, tc=tc: e.tensor_tensor(
                                   out=gub[pb][:, tc * 512:(tc + 1) * 512],
                                   in0=gateb[:, tc, :], in1=upb[:, tc, :], op=ALU.mult))
                t_guv4[tc] = t_guv
            t_gust[f] = b.task("sync", [t_guv], lambda e, f=f, pb=pb: e.dma_start(
                out=gu_dram[f], in_=gub[pb][:]), dma=True)
            if "ga_dbg" in dbg:
                b.task("gpsimd", [t_guv], lambda e, f=f, pb=pb: e.dma_start(
                    out=dbg["ga_dbg"][f], in_=gub[pb][:]), dma=True)
        state["T_GU"] = t_gust[NF - 1]
        free("pqJ3", "pqJ2", "pqJ1", "pqJ0", "pgJ3", "pgJ2", "pgJ1", "pgJ0", "gub1", "gub0", "upb", "gateb",
             "wupb1", "wupb0", "winb1", "winb0", "b_sb", "modA", "hA_sb", "h_sb")

    if "K" in phases:
        b.barrier()
        gu_sb = sbuf("gu_sb", [128, NF, 512], BF16)
        wdnb = [sbuf("wdnb0", [128, NF, 128], BF16), sbuf("wdnb1", [128, NF, 128], BF16)]
        o1k = [sbuf("o1k0", [128, 512], F32), sbuf("o1k1", [128, 512], F32)]
        mbK = sbuf("mbK", [128, 512], F32)
        rbK = sbuf("rbK", [128, 512], F32)
        t1K = sbuf("t1K", [128, 512], F32)
        finb = [sbuf("finb0", [128, 512], F32), sbuf("finb1", [128, 512], F32)]
        pK = [psum("pK0", [128, 512], F32), psum("pK1", [128, 512], F32)]
        pbK = psum("pbK", [128, 512], F32)

        t_fin = {}
        t_fst = {}
        t_guld = [None] * NT
        t_wdld = {}
        t_mmN = [None, None]
        t_bcev = None
        for tb in range(NT):
            t_guld[tb] = b.task("sync", [t_fst.get((tb - 1, ND - 1)), state["T_GU"] if tb == 0 else None],
                                lambda e, tb=tb: e.dma_start(
                                    out=gu_sb[:], in_=gu_dram[:, :, tb * 512:(tb + 1) * 512]
                                    .rearrange("f p t -> p f t")), dma=True)
            t_bm = b.task("tensor", [t_bcev, state["T_STATS2"] if tb == 0 else None,
                                     t_fin.get((tb - 1, ND - 1))],
                          lambda e, tb=tb: e.matmul(
                              pbK[:, :], ones_f32[0:1, :],
                              state["m2_sb"][0:1, tb * 512:(tb + 1) * 512], start=True, stop=True))
            t_bmev = b.task("vector", [t_bm], lambda e: e.tensor_copy(out=mbK[:], in_=pbK[:, :]))
            t_br = b.task("tensor", [t_bmev], lambda e, tb=tb: e.matmul(
                pbK[:, :], ones_f32[0:1, :],
                state["r2_sb"][0:1, tb * 512:(tb + 1) * 512], start=True, stop=True))
            t_bcev = b.task("vector", [t_br], lambda e: e.tensor_copy(out=rbK[:], in_=pbK[:, :]))
            for d in range(ND):
                pb = d % 2
                bank = d % 2
                t_wdld[(tb, d)] = b.task("gpsimd", [t_mmN[pb]],
                                         lambda e, tb=tb, d=d, pb=pb: e.dma_start(
                                             out=wdnb[pb][:],
                                             in_=wdnbf_dram[d].rearrange(
                                                 "p (f c) -> p f c", f=NF)), dma=True)
                t_o1l = b.task("sync", [t_fin.get((tb, d - 2))],
                               lambda e, tb=tb, d=d, pb=pb: e.dma_start(
                                   out=o1k[pb][:], in_=out1_dram[d][:, tb * 512:(tb + 1) * 512]),
                               dma=True)

                def em_kmm(e, tb=tb, d=d, pb=pb, bank=bank):
                    last = None
                    for f in range(NF):
                        last = e.matmul(pK[bank][:, :], wdnb[pb][:, f, :],
                                        gu_sb[:, f, :], start=(f == 0), stop=(f == NF - 1))
                    return last
                t_kmm = b.task("tensor", [t_wdld[(tb, d)], t_guld[tb],
                                          t_fin.get((tb, d - 2))], em_kmm)
                t_mmN[pb] = t_kmm

                def em_fin(e, tb=tb, d=d, pb=pb, bank=bank):
                    e.tensor_tensor(out=t1K[:], in0=o1k[pb][:], in1=mbK[:], op=ALU.subtract)
                    e.tensor_tensor(out=t1K[:], in0=t1K[:], in1=rbK[:], op=ALU.mult)
                    e.tensor_tensor(out=t1K[:], in0=t1K[:], in1=o1k[pb][:], op=ALU.add)
                    return e.tensor_tensor(out=finb[pb][:], in0=t1K[:], in1=pK[bank][:, :],
                                           op=ALU.add)
                t_fin[(tb, d)] = b.task("vector", [t_kmm, t_o1l, t_bcev,
                                                   t_fst.get((tb, d - 2))], em_fin)
                t_fst[(tb, d)] = b.task("sync", [t_fin[(tb, d)]],
                                        lambda e, tb=tb, d=d, pb=pb: e.dma_start(
                                            out=out_ext[d * 128:(d + 1) * 128,
                                                        tb * 512:(tb + 1) * 512],
                                            in_=finb[pb][:]), dma=True)
        b.barrier()

    state["dbg"] = dbg
    return nc, b, state



def _pcol(v):
    v = np.asarray(v)
    n = v.shape[0]
    return np.ascontiguousarray(v.reshape(n // 128, 128).T)


def stage_inputs(inp):
    x = np.asarray(inp["L0_attn_x"], dtype=np.float32)
    shared = {
        "wu_t": np.ascontiguousarray(np.asarray(inp["L0_attn_Wu"]).T),
        "wv_t": np.ascontiguousarray(np.asarray(inp["L0_attn_Wv"]).T),
        "wg1_t": np.ascontiguousarray(np.asarray(inp["L0_attn_Wg1"]).T),
        "wg2_t": np.ascontiguousarray(np.asarray(inp["L0_attn_Wg2"]).T),
        "bu": _pcol(inp["L0_attn_bu"]),
        "bv": _pcol(inp["L0_attn_bv"]),
        "bg1": _pcol(inp["L0_attn_bg1"]),
        "bg2": _pcol(inp["L0_attn_bg2"]),
        "convw": np.ascontiguousarray(
            np.asarray(inp["L0_attn_conv_w"])[:, 0, :].reshape(ND, 128, 3).transpose(1, 0, 2)),
        "convb": _pcol(inp["L0_attn_conv_b"]),
        "ln1w": _pcol(inp["L0_ffn_ln1_w"]),
        "ln2w": _pcol(inp["L0_ffn_ln2_w"]),
        "a_mat": np.ascontiguousarray(np.asarray(inp["L0_ffn_A"])),
        "b_mat": np.ascontiguousarray(np.asarray(inp["L0_ffn_B"])),
        "win_t": np.ascontiguousarray(np.asarray(inp["L0_ffn_W_in_base"]).T),
        "wup_t": np.ascontiguousarray(np.asarray(inp["L0_ffn_W_up_base"]).T),
        "wdn_t": np.ascontiguousarray(np.asarray(inp["L0_ffn_W_down"]).T),
    }
    wq = np.asarray(inp["L0_ffn_Wq"])
    wk = np.asarray(inp["L0_ffn_Wk"])
    wv2 = np.asarray(inp["L0_ffn_Wv"])
    wo = np.asarray(inp["L0_ffn_Wo"])
    kk = np.arange(128)[:, None]
    qq = np.arange(512)[None, :]
    masks = np.stack([(128 * j + kk <= qq).astype(np.float32) for j in range(4)], axis=1)
    shared["masks"] = np.ascontiguousarray(masks)
    m2p = np.stack([np.concatenate([masks[:, 2 * m, :], masks[:, 2 * m + 1, :]], axis=1)
                    for m in range(2)], axis=1)
    shared["masks2"] = np.ascontiguousarray(m2p)
    shared["ident"] = np.eye(128, dtype=np.float32)
    shared["selh"] = np.ascontiguousarray(np.tile(np.eye(16, dtype=np.float32), (4, 1)))

    in_maps = []
    for c in range(8):
        bidx, rank = c // 4, c % 4
        chunk = x[bidx, rank * T:(rank + 1) * T, :]
        x_ext = np.zeros((D, TX), dtype=np.float32)
        x_ext[:, 2:2 + T] = chunk.T
        if rank > 0:
            x_ext[:, 1] = x[bidx, rank * T - 1, :]
        if rank < 3:
            x_ext[:, 2 + T] = x[bidx, (rank + 1) * T, :]
        sel = np.zeros((4, 1), dtype=np.float32)
        sel[:rank] = 1.0
        hs = slice(512 * rank, 512 * (rank + 1))
        m = dict(shared)
        m.update({
            "x_ext": x_ext,
            "sel": sel,
            "wq_t": np.ascontiguousarray(wq.T[:, hs]),
            "wk_t": np.ascontiguousarray(wk.T[:, hs]),
            "wv2_t": np.ascontiguousarray(wv2.T[:, hs]),
            "wo_sl": np.ascontiguousarray(wo[:, hs]),
        })
        in_maps.append(m)
    return in_maps


def kernel(**inputs):
    in_maps = stage_inputs(inputs)
    nc, b, state = build_program()
    b.emit()
    res = run_bass_kernel_spmd(nc, in_maps, core_ids=list(range(8)))
    out = np.empty((2, S, D), dtype=np.float32)
    for c in range(8):
        bidx, rank = c // 4, c % 4
        out[bidx, rank * T:(rank + 1) * T, :] = res.results[c]["out"].T
    return out


# version 21
# speedup vs baseline: 1.2243x; 1.2243x over previous
import numpy as np
import ml_dtypes
import concourse.bass as bass
import concourse.mybir as mybir
from concourse.bass_utils import run_bass_kernel_spmd

F32 = mybir.dt.float32
BF16 = mybir.dt.bfloat16
AF = mybir.ActivationFunctionType
ALU = mybir.AluOpType

D = 2048
ND = 16
T = 2048
TX = 2052
NT = 4
S = 8192
R = 2048
NR = 16
H1 = 512
NH1 = 4
NHEAD = 4
HD = 128
FF = 8192
NF = 64
GROUPS = [[0, 1, 2, 3], [4, 5, 6, 7]]
ISQH = float(1.0 / np.sqrt(128.0))
A_CHUNKS = [(0, 512), (512, 512), (1024, 512), (1536, 512), (2048, 4)]
T_CHUNKS = [(0, 512), (512, 512), (1024, 512), (1536, 512)]

ENGINES = ("sync", "gpsimd", "tensor", "vector", "scalar")


class Builder:

    NROT = 6

    def __init__(self, nc):
        self.nc = nc
        self.prog = {e: [] for e in ENGINES}
        self.cnt = {}
        self.handles = {}
        self._sem_cms = []
        self._rr = {"dS": 0, "dG": 0}
        self._ccn = 0
        names = ["qS", "qG", "qT", "qV", "qA"] + [f"cc{i}" for i in range(8)]
        names += [f"dS{i}" for i in range(self.NROT)]
        names += [f"dG{i}" for i in range(self.NROT)]
        for name in names:
            cm = nc.semaphore(name)
            h = cm.__enter__()
            self._sem_cms.append(cm)
            self.handles[name] = h
            self.cnt[name] = 0

    def _engine_sem(self, eng, dma):
        if dma:
            base = {"sync": "dS", "gpsimd": "dG"}[eng]
            i = self._rr[base]
            self._rr[base] = (i + 1) % self.NROT
            return f"{base}{i}"
        return {"sync": "qS", "gpsimd": "qG", "tensor": "qT", "vector": "qV", "scalar": "qA"}[eng]

    def task(self, eng, deps, emit_fn, dma=False, inc=None, multi=False, use_cc=False):
        if use_cc:
            sem_name = f"cc{self._ccn}"
            self._ccn += 1
        else:
            sem_name = self._engine_sem(eng, dma)
        amount = 16 if dma else 1
        if inc is not None:
            amount = inc
        self.cnt[sem_name] += amount
        target = self.cnt[sem_name]
        sem = self.handles[sem_name]
        deps = [d for d in deps if d is not None]
        handles = self.handles

        def run(e, deps=tuple(deps), emit_fn=emit_fn, sem=sem, amount=amount, multi=multi):
            for dn, dv in deps:
                e.wait_ge(handles[dn], dv)
            if multi:
                emit_fn(e, sem)
            else:
                last = emit_fn(e)
                if last is not None:
                    last.then_inc(sem, amount)

        self.prog[eng].append(run)
        return (sem_name, target)

    def wait(self, eng, deps):
        handles = self.handles

        def run(e, deps=tuple(d for d in deps if d is not None)):
            for dn, dv in deps:
                e.wait_ge(handles[dn], dv)

        self.prog[eng].append(run)

    def barrier(self):
        snapshot = [(n, self.cnt[n]) for n in self.cnt if self.cnt[n] > 0]
        for eng in ("sync", "gpsimd", "tensor", "vector", "scalar"):
            self.task(eng, snapshot, lambda e: e.nop())

    def emit(self):
        nc = self.nc
        with nc.Block() as block:
            @block.sync
            def _(e):
                for f in self.prog["sync"]:
                    f(e)

            @block.gpsimd
            def _(e):
                for f in self.prog["gpsimd"]:
                    f(e)

            @block.tensor
            def _(e):
                for f in self.prog["tensor"]:
                    f(e)

            @block.vector
            def _(e):
                for f in self.prog["vector"]:
                    f(e)

            @block.scalar
            def _(e):
                for f in self.prog["scalar"]:
                    f(e)
        return nc


def build_program(dev=(), phases=("A", "B", "C", "D", "E", "F", "M", "H", "I", "J", "K")):
    dev = set(dev)
    phases = set(phases)
    nc = bass.Bass("TRN2", target_bir_lowering=False)
    b = Builder(nc)
    ctxs = {}

    def dram_in(name, shape, dtype=F32):
        return nc.dram_tensor(name, shape, dtype, kind="ExternalInput").ap()

    def scratch(name, shape, dtype):
        kind = "ExternalOutput" if name in dev else "Internal"
        return nc.dram_tensor(name, shape, dtype, kind=kind).ap()

    def sbuf(name, shape, dtype, side=None):
        cm = nc.sbuf_tensor(name, shape, dtype, side=side) if side else nc.sbuf_tensor(name, shape, dtype)
        t = cm.__enter__()
        ctxs[name] = cm
        return t

    def psum(name, shape, dtype=F32):
        cm = nc.psum_tensor(name, shape, dtype)
        t = cm.__enter__()
        ctxs[name] = cm
        return t

    def free(*names):
        order = list(ctxs.keys())
        for n in sorted(names, key=lambda n: -order.index(n)):
            ctxs.pop(n).__exit__(None, None, None)

    x_ext = dram_in("x_ext", [D, TX])
    wu_t = dram_in("wu_t", [D, R])
    wv_t = dram_in("wv_t", [D, R])
    wg1_t = dram_in("wg1_t", [D, H1])
    wg2_t = dram_in("wg2_t", [H1, R])
    bu_in = dram_in("bu", [128, NR])
    bv_in = dram_in("bv", [128, NR])
    bg1_in = dram_in("bg1", [128, NH1])
    bg2_in = dram_in("bg2", [128, NR])
    convw_in = dram_in("convw", [128, ND, 3])
    convb_in = dram_in("convb", [128, ND])
    ln1w_in = dram_in("ln1w", [128, ND])
    ln2w_in = dram_in("ln2w", [128, ND])
    wq_t = dram_in("wq_t", [D, 512])
    wk_t = dram_in("wk_t", [D, 512])
    wv2_t = dram_in("wv2_t", [D, 512])
    wo_sl = dram_in("wo_sl", [D, 512])
    a_in = dram_in("a_mat", [D, 16])
    b_in = dram_in("b_mat", [16, FF])
    win_t = dram_in("win_t", [D, FF])
    wup_t = dram_in("wup_t", [D, FF])
    wdn_t = dram_in("wdn_t", [FF, D])
    masks_in = dram_in("masks", [128, 4, 512])
    masks2_in = dram_in("masks2", [128, 2, 1024])
    sel_in = dram_in("sel", [4, 1])
    ident_in = dram_in("ident", [128, 128])
    selh_in = dram_in("selh", [64, 16])

    out_ext = nc.dram_tensor("out", [D, T], F32, kind="ExternalOutput").ap()

    xlf_dram = scratch("xlf", [ND, 128, T], F32)
    cs_dram = scratch("cs", [NR, 128, T], BF16)
    out1_dram = scratch("out1", [ND, 128, T], F32)
    h_dram = scratch("h_mlp", [ND, 128, T], BF16)
    gu_dram = scratch("gu", [NF, 128, T], BF16)
    ag1_in = scratch("ag1_in", [D, T], BF16)
    ag1_out = scratch("ag1_out", [4 * D, T], BF16)
    ag2_in = scratch("ag2_in", [128, NR], F32)
    ag2_out = scratch("ag2_out", [4 * 128, NR], F32)
    pref_bounce = scratch("pref_bounce", [1, 128 * NR], F32)
    rs3_in = scratch("rs3_in", [64, T], F32)
    rs3_out = scratch("rs3_out", [16, T], F32)
    modh_dram = scratch("modh", [NHEAD, 16, S], F32)
    wdnbf_dram = scratch("wdnbf", [ND, 128, NF * 128], BF16)
    dbg = {}
    for name, shape, dt in [
        ("y_dbg", [ND, 128, TX], F32), ("h1_dbg", [NH1, 128, T], F32),
        ("ug_dbg", [NR, 128, T], F32), ("tot_dbg", [128, NR], F32),
        ("pref_dbg", [128, NR], F32), ("glob_dbg", [NR, 128, T], F32),
        ("stats_dbg", [4, T], F32), ("xln1_dbg", [ND, 128, T], F32),
        ("mod_dbg", [16, S], F32), ("q_dbg", [NHEAD, 128, S], F32),
        ("vaug_dbg", [NHEAD, 128, 64 * 132], F32), ("o_dbg", [NHEAD, 128, S], F32),
        ("modf_dbg", [16, T], F32), ("ga_dbg", [NF, 128, T], F32),
        ("up_dbg", [NF, 128, T], F32), ("bh_dbg", [NHEAD, 1], F32),
        ("ysb_dbg", [ND, 128, TX], F32), ("msb_dbg", [128, NHEAD, 16], F32),
    ]:
        if name in dev:
            dbg[name] = scratch(name, shape, dt)

    bu_sb = sbuf("bu_sb", [128, NR], F32)
    bv_sb = sbuf("bv_sb", [128, NR], F32)
    bg1_sb = sbuf("bg1_sb", [128, NH1], F32)
    bg2_sb = sbuf("bg2_sb", [128, NR], F32)
    convw_sb = sbuf("convw_sb", [128, ND, 3], F32)
    convb_sb = sbuf("convb_sb", [128, ND], F32)
    ln1w_sb = sbuf("ln1w_sb", [128, ND], F32)
    ln2w_sb = sbuf("ln2w_sb", [128, ND], F32)
    ones_f32 = sbuf("ones_f32", [1, 128], F32)
    ones_bf = sbuf("ones_bf", [1, 128], BF16)
    onesc_bf = sbuf("onesc_bf", [128, 1], BF16)
    onesc_f32 = sbuf("onesc_f32", [128, 1], F32)
    masks2_sb = sbuf("masks2_sb", [128, 2, 1024], BF16)
    sel_sb = sbuf("sel_sb", [4, 1], F32)
    ident_sb = sbuf("ident_sb", [128, 128], BF16)
    selh_sb = sbuf("selh_sb", [64, 16], F32)

    def _pre_dmas(e, sem):
        for dst, src in [(bu_sb, bu_in), (bv_sb, bv_in), (bg1_sb, bg1_in),
                         (bg2_sb, bg2_in), (convb_sb, convb_in), (ln1w_sb, ln1w_in),
                         (convw_sb, convw_in), (ln2w_sb, ln2w_in), (sel_sb, sel_in)]:
            e.dma_start(out=dst[:], in_=src).then_inc(sem, 16)
    state = {}
    t_pre1 = b.task("sync", [], _pre_dmas, dma=True, inc=16 * 9, multi=True)
    t_pre2 = b.task("gpsimd", [], lambda e:
                     e.dma_start(out=masks2_sb[:], in_=masks2_in), dma=True)
    t_pre2b = b.task("gpsimd", [], lambda e:
                     e.dma_start(out=ident_sb[:], in_=ident_in), dma=True)
    t_pre2c = b.task("sync", [], lambda e:
                     e.dma_start(out=selh_sb[:], in_=selh_in), dma=True)
    state["T_IDENT"] = t_pre2b
    state["T_SELH"] = t_pre2c
    t_pre3 = b.task("gpsimd", [], lambda e: [
        e.memset(ones_f32[:], 1.0),
        e.memset(ones_bf[:], 1.0),
        e.memset(onesc_bf[:], 1.0),
        e.memset(onesc_f32[:], 1.0)][-1])
    PRE = [t_pre1, t_pre2, t_pre3]


    if "A" in phases:
        y_sb = sbuf("y_sb", [128, ND, TX], BF16, side="right")
        xbf = y_sb
        xd = [sbuf("xd0", [128, TX], F32), sbuf("xd1", [128, TX], F32)]
        xsq = [sbuf("xsq0", [128, TX], BF16), sbuf("xsq1", [128, TX], BF16)]
        stp = [psum(f"stp{i}", [64, 512], F32) for i in range(5)]
        m1_sb = sbuf("m1_sb", [1, TX], F32)
        r1_sb = sbuf("r1_sb", [1, TX], F32)
        tmp_a = sbuf("tmp_a", [1, TX], F32)
        mbc = sbuf("mbc", [128, TX], F32)
        rbc = sbuf("rbc", [128, TX], F32)
        bcp = psum("bcp", [128, 512], F32)
        yf = [sbuf("yf0", [128, TX], F32), sbuf("yf1", [128, TX], F32)]
        cvs = [sbuf("cvs0", [128, T], F32), sbuf("cvs1", [128, T], F32)]
        cv1 = sbuf("cv1", [128, T], F32)
        xlf_sb = [sbuf("xlf0", [128, T], F32), sbuf("xlf1", [128, T], F32)]

        t_sq = [None] * ND
        t_mmst = [None] * ND
        t_xbf = [None, None]
        for half in range(2):
            t_xbf[half] = b.task("gpsimd", PRE, lambda e, half=half: e.dma_start(
                out=xbf[:, half * 8:(half + 1) * 8, :],
                in_=x_ext[half * 1024:(half + 1) * 1024, :]
                .rearrange("(c p) t -> p c t", p=128)), dma=True)
        for d in range(ND):
            pb = d % 2
            t_sq[d] = b.task("scalar", [t_xbf[d // 8], t_mmst[d - 2] if d >= 2 else None],
                             lambda e, d=d, pb=pb: e.activation(
                                 out=xsq[pb][:], in_=xbf[:, d, :], func=AF.Square))

            def mm_stats(e, d=d, pb=pb):
                last = None
                for i, (c0, w) in enumerate(A_CHUNKS):
                    e.matmul(stp[i][0:1, 0:w], onesc_bf[:, :], xbf[:, d, c0:c0 + w],
                             start=(d == 0), stop=(d == ND - 1))
                    last = e.matmul(stp[i][32:33, 0:w], onesc_bf[:, :], xsq[pb][:, c0:c0 + w],
                                    start=(d == 0), stop=(d == ND - 1))
                return last
            t_mmst[d] = b.task("tensor", [t_sq[d]], mm_stats)

        def stats_math_v(e):
            for i, (c0, w) in enumerate(A_CHUNKS):
                e.tensor_scalar(out=m1_sb[0:1, c0:c0 + w], in0=stp[i][0:1, 0:w],
                                scalar1=1.0 / D, scalar2=None, op0=ALU.mult)
                e.tensor_scalar(out=tmp_a[0:1, c0:c0 + w], in0=stp[i][32:33, 0:w],
                                scalar1=1.0 / D, scalar2=None, op0=ALU.mult)
            e.tensor_tensor(out=r1_sb[0:1, :], in0=m1_sb[0:1, :], in1=m1_sb[0:1, :], op=ALU.mult)
            return e.scalar_tensor_tensor(out=r1_sb[0:1, :], in0=tmp_a[0:1, :], scalar=1e-5,
                                          op0=ALU.add, in1=r1_sb[0:1, :], op1=ALU.subtract)
        t_vstat = b.task("vector", [t_mmst[ND - 1]], stats_math_v)
        t_sstat = b.task("scalar", [t_vstat],
                         lambda e: e.activation(out=tmp_a[0:1, :], in_=r1_sb[0:1, :], func=AF.Sqrt))
        t_rstd = b.task("vector", [t_sstat],
                        lambda e: e.reciprocal(out=r1_sb[0:1, :], in_=tmp_a[0:1, :]))

        t_bc_mm = t_rstd
        t_bc_ev = None
        for j, (src, dst) in enumerate([(m1_sb, mbc), (r1_sb, rbc)]):
            for i, (c0, w) in enumerate(A_CHUNKS):
                t_bc_mm = b.task("tensor", [t_bc_mm, t_bc_ev],
                                 lambda e, src=src, c0=c0, w=w: e.matmul(
                                     bcp[:, 0:w], ones_f32[0:1, :], src[0:1, c0:c0 + w],
                                     start=True, stop=True))
                t_bc_ev = b.task("vector", [t_bc_mm],
                                 lambda e, dst=dst, c0=c0, w=w: e.tensor_copy(
                                     out=dst[:, c0:c0 + w], in_=bcp[:, 0:w]))
        t_bc_done = t_bc_ev

        t_y = [None] * ND
        t_yc = [None] * ND
        t_c0 = [None] * ND
        t_xlf = [None] * ND
        t_st = [None] * ND
        t_ld2 = [None] * ND
        for d in range(ND):
            pb = d % 2
            t_ld2[d] = b.task("sync", [t_bc_done if d < 2 else None,
                                       t_xlf[d - 2] if d >= 2 else None],
                              lambda e, d=d, pb=pb: e.dma_start(
                                  out=xd[pb][:], in_=x_ext[d * 128:(d + 1) * 128, :]),
                              dma=True)

            def em_y(e, d=d, pb=pb):
                e.tensor_tensor(out=yf[pb][:], in0=xbf[:, d, :], in1=mbc[:], op=ALU.subtract)
                return e.tensor_tensor(out=yf[pb][:], in0=yf[pb][:], in1=rbc[:], op=ALU.mult)
            t_y[d] = b.task("vector", [t_bc_done if d < 2 else None,
                                       t_yc[d - 2] if d >= 2 else None], em_y)

            def em_sc(e, d=d, pb=pb):
                e.activation(out=cvs[pb][:], in_=yf[pb][:, 1:2049], func=AF.Identity,
                             bias=convb_sb[:, d:d + 1], scale=convw_sb[:, d, 0:1])
                return e.copy(out=y_sb[:, d, :], in_=yf[pb][:])
            t_c0[d] = b.task("scalar", [t_y[d], t_xlf[d - 2] if d >= 2 else None], em_sc)
            t_yc[d] = t_c0[d]

            def em_conv(e, d=d, pb=pb):
                e.scalar_tensor_tensor(out=cv1[:], in0=yf[pb][:, 2:2050],
                                       scalar=convw_sb[:, d, 1:2], op0=ALU.mult,
                                       in1=cvs[pb][:], op1=ALU.add)
                e.scalar_tensor_tensor(out=cvs[pb][:], in0=yf[pb][:, 3:2051],
                                       scalar=convw_sb[:, d, 2:3], op0=ALU.mult,
                                       in1=cv1[:], op1=ALU.add)
                return e.tensor_tensor(out=xlf_sb[pb][:], in0=xd[pb][:, 2:2050],
                                       in1=cvs[pb][:], op=ALU.add)
            t_xlf[d] = b.task("vector", [t_c0[d], t_ld2[d], t_st[d - 2] if d >= 2 else None], em_conv)
            t_st[d] = b.task("sync", [t_xlf[d]],
                             lambda e, d=d, pb=pb: e.dma_start(out=xlf_dram[d], in_=xlf_sb[pb][:]),
                             dma=True)
            if "y_dbg" in dbg:
                t_ydbg = b.task("gpsimd", [t_c0[d]],
                                lambda e, d=d, pb=pb: e.dma_start(out=dbg["y_dbg"][d], in_=yf[pb][:]),
                                dma=True)
        t_ysbdbg = None
        if "ysb_dbg" in dbg:
            for d in range(ND):
                t_ysbdbg = b.task("gpsimd", [t_yc[ND - 1]], lambda e, d=d: e.dma_start(
                    out=dbg["ysb_dbg"][d], in_=y_sb[:, d, :]), dma=True)
        state["t_y_done"] = [t_yc[ND - 1], t_st[ND - 1]]
        state["A_DONE"] = [t_yc[ND - 1], t_xlf[ND - 1], t_st[ND - 1], t_st[ND - 2], t_ysbdbg]
        state["y_sb"] = y_sb
        state["t_xlf_all"] = t_st
        free("xd0", "xd1", "xsq0", "xsq1", "stp0", "stp1", "stp2", "stp3", "stp4",
             "bcp", "yf0", "yf1", "cvs0", "cvs1", "cv1", "xlf0", "xlf1",
             "mbc", "rbc", "m1_sb", "r1_sb", "tmp_a")

    if "B" in phases:
        dep_y = state["t_y_done"]
        wg2_sb = sbuf("wg2_sb", [128, NH1, R], BF16, side="right")
        h1_sb = sbuf("h1_sb", [128, NH1, T], BF16, side="right")
        totals_sb = sbuf("totals_sb", [128, NR], F32, side="right")
        wg1_sb = sbuf("wg1_sb", [128, ND, 512], BF16)
        phB = [psum("phB0", [128, 512], F32), psum("phB1", [128, 512], F32)]

        A_DONE = state["A_DONE"]
        t_wg1 = b.task("gpsimd", PRE + A_DONE, lambda e: e.dma_start(
            out=wg1_sb[:], in_=wg1_t.rearrange("(c p) h -> p c h", p=128)), dma=True)
        t_wg2 = b.task("gpsimd", PRE + A_DONE, lambda e: e.dma_start(
            out=wg2_sb[:], in_=wg2_t.rearrange("(c p) r -> p c r", p=128)), dma=True)

        t_bev = [None, None]
        t_h1 = []
        for hc in range(NH1):
            for tc in range(NT):
                bank = (hc * NT + tc) % 2

                def em_bmm(e, hc=hc, tc=tc, bank=bank):
                    last = None
                    for d in range(ND):
                        last = e.matmul(phB[bank][:, :], wg1_sb[:, d, hc * 128:(hc + 1) * 128],
                                        y_sb[:, d, 2 + tc * 512:2 + (tc + 1) * 512],
                                        start=(d == 0), stop=(d == ND - 1))
                    return last
                t_bmm = b.task("tensor", dep_y + [t_wg1, t_bev[bank]], em_bmm)
                t_bmm_last = t_bmm
                t_bev[bank] = b.task("scalar", [t_bmm],
                                     lambda e, hc=hc, tc=tc, bank=bank: e.activation(
                                         out=h1_sb[:, hc, tc * 512:(tc + 1) * 512],
                                         in_=phB[bank][:, :], func=AF.Relu,
                                         bias=bg1_sb[:, hc:hc + 1]))
                t_h1.append(t_bev[bank])
        T_H1 = t_h1[-1]
        if "h1_dbg" in dbg:
            for hc in range(NH1):
                b.task("gpsimd", [T_H1], lambda e, hc=hc: e.dma_start(
                    out=dbg["h1_dbg"][hc], in_=h1_sb[:, hc, :]), dma=True)
        free("wg1_sb", "phB0", "phB1")
        state["h1_sb"] = h1_sb
        state["wg2_sb"] = wg2_sb
        state["totals_sb"] = totals_sb
        state["T_H1"] = T_H1
        state["B_MM_DONE"] = t_bmm_last

    if "C" in phases:
        h1_sb = state["h1_sb"]
        wg2_sb = state["wg2_sb"]
        totals_sb = state["totals_sb"]
        T_H1 = state["T_H1"]
        wub = [sbuf("wub0", [128, ND, 128], BF16), sbuf("wub1", [128, ND, 128], BF16)]
        u_sb = sbuf("u_sb", [128, T], F32)
        g_sb = sbuf("g_sb", [128, T], BF16)
        ug_sb = sbuf("ug_sb", [128, T], BF16)
        cso = [sbuf("cso0", [128, T], BF16), sbuf("cso1", [128, T], BF16)]
        puC = [psum("puC0", [128, 512], F32), psum("puC1", [128, 512], F32)]
        pgC = [psum("pgC0", [128, 512], F32), psum("pgC1", [128, 512], F32)]

        t_wuld = [None] * NR
        t_ummN = [None, None]
        t_uevN = [None, None]
        t_gevN = [None, None]
        t_ug = [None] * NR
        t_scan = [None] * NR
        t_csst = [None] * NR
        for r in range(NR):
            pb = r % 2
            t_wuld[r] = b.task("gpsimd", PRE + [t_ummN[pb], state["B_MM_DONE"] if r < 2 else None],
                               lambda e, r=r, pb=pb: e.dma_start(
                                   out=wub[pb][:], in_=wu_t[:, r * 128:(r + 1) * 128]
                                   .rearrange("(c p) r -> p c r", p=128)), dma=True)
            b.task("gpsimd", [], lambda e, r=r: e.dma_start(
                out=wdnbf_dram[r].rearrange("p (f c) -> p f c", f=NF),
                in_=wdn_t[:, r * 128:(r + 1) * 128]
                .rearrange("(f p) c -> p f c", p=128)), dma=True)
            t_uev = [None] * NT
            t_gev = [None] * NT
            for tc in range(NT):
                bank = tc % 2

                def em_umm(e, r=r, pb=pb, tc=tc, bank=bank):
                    last = None
                    for d in range(ND):
                        last = e.matmul(puC[bank][:, :], wub[pb][:, d, :],
                                        y_sb[:, d, 2 + tc * 512:2 + (tc + 1) * 512],
                                        start=(d == 0), stop=(d == ND - 1))
                    return last
                t_umm = b.task("tensor", [t_wuld[r], t_uevN[bank]] +
                               (state["t_y_done"] + [T_H1] if r == 0 else []) +
                               [t_ug[r - 1] if r >= 1 and tc == 0 else None], em_umm)
                t_ummN[pb] = t_umm
                t_uev[tc] = b.task("scalar", [t_umm],
                                   lambda e, r=r, tc=tc, bank=bank: e.activation(
                                       out=u_sb[:, tc * 512:(tc + 1) * 512],
                                       in_=puC[bank][:, :], func=AF.Identity,
                                       bias=bu_sb[:, r:r + 1]))
                t_uevN[bank] = t_uev[tc]

                def em_gmm(e, r=r, tc=tc, bank=bank):
                    last = None
                    for hc in range(NH1):
                        last = e.matmul(pgC[bank][:, :], wg2_sb[:, hc, r * 128:(r + 1) * 128],
                                        h1_sb[:, hc, tc * 512:(tc + 1) * 512],
                                        start=(hc == 0), stop=(hc == NH1 - 1))
                    return last
                t_gmm = b.task("tensor", [T_H1 if r == 0 else None, t_gevN[bank],
                                          t_ug[r - 1] if r >= 1 and tc == 0 else None], em_gmm)
                t_gev[tc] = b.task("scalar", [t_gmm],
                                   lambda e, r=r, tc=tc, bank=bank: e.activation(
                                       out=g_sb[:, tc * 512:(tc + 1) * 512],
                                       in_=pgC[bank][:, :], func=AF.Sigmoid,
                                       bias=bg2_sb[:, r:r + 1]))
                t_gevN[bank] = t_gev[tc]
            t_ug[r] = b.task("vector", [t_uev[NT - 1], t_gev[NT - 1]],
                             lambda e, r=r: e.scalar_tensor_tensor(
                                 out=ug_sb[:], in0=u_sb[:], scalar=1.0, op0=ALU.mult,
                                 in1=g_sb[:], op1=ALU.mult,
                                 accum_out=totals_sb[:, r:r + 1]))
            t_scan[r] = b.task("vector", [t_csst[r - 2] if r >= 2 else None],
                               lambda e, r=r, pb=pb: e.tensor_tensor_scan(
                                   out=cso[pb][:], data0=ug_sb[:], data1=ug_sb[:],
                                   initial=0.0, op0=ALU.add, op1=ALU.bypass))
            t_csst[r] = b.task("sync", [t_scan[r]],
                               lambda e, r=r, pb=pb: e.dma_start(
                                   out=cs_dram[r], in_=cso[pb][:]), dma=True)
            if "ug_dbg" in dbg:
                b.task("gpsimd", [t_scan[r]], lambda e, r=r: e.dma_start(
                    out=dbg["ug_dbg"][r], in_=ug_sb[:]), dma=True)
        T_TOTALS = t_scan[NR - 1]
        T_CS = t_csst[NR - 1]
        if "tot_dbg" in dbg:
            b.task("gpsimd", [T_TOTALS], lambda e: e.dma_start(
                out=dbg["tot_dbg"], in_=totals_sb[:]), dma=True)
        state["T_TOTALS"] = T_TOTALS
        state["T_CS"] = T_CS
        state["t_wu_war"] = t_ummN
        free("wub0", "wub1", "u_sb", "g_sb", "ug_sb", "cso0", "cso1",
             "puC0", "puC1", "pgC0", "pgC1")

    if "D" in phases:
        b.barrier()
        pref_sb = sbuf("pref_sb", [128, NR], F32)
        g2_sb = sbuf("g2_sb", [4, 128, NR], F32)
        prefflat = sbuf("prefflat", [1, 128 * NR], F32)
        prefp = psum("prefp", [1, 512], F32)

        t_ag2w = b.task("sync", [], lambda e: e.dma_start(
            out=ag2_in, in_=state["totals_sb"][:]), dma=True)
        t_ag2 = b.task("gpsimd", [t_ag2w], lambda e: e.collective_compute(
            "AllGather", ALU.bypass, replica_groups=GROUPS,
            ins=[ag2_in.opt()], outs=[ag2_out.opt()]), use_cc=True)
        t_g2ld = b.task("sync", [t_ag2], lambda e: e.dma_start(
            out=g2_sb[:], in_=ag2_out.rearrange("(r p) c -> r p c", p=128)), dma=True)
        t_pev = None
        for i in range(4):
            t_pmm = b.task("tensor", [t_g2ld, t_pev],
                           lambda e, i=i: e.matmul(
                               prefp[:, :], sel_sb[:, :],
                               g2_sb[:, i * 32:(i + 1) * 32, :], start=True, stop=True))
            t_pev = b.task("vector", [t_pmm],
                           lambda e, i=i: e.tensor_copy(
                               out=prefflat[0:1, i * 512:(i + 1) * 512], in_=prefp[:, :]))
        t_pfw = b.task("sync", [t_pev], lambda e: e.dma_start(
            out=pref_bounce, in_=prefflat[0:1, :]), dma=True)
        t_pfr = b.task("sync", [t_pfw], lambda e: e.dma_start(
            out=pref_sb[:], in_=pref_bounce.rearrange("o (p c) -> (o p) c", p=128)), dma=True)
        if "pref_dbg" in dbg:
            b.task("gpsimd", [t_pfr], lambda e: e.dma_start(
                out=dbg["pref_dbg"], in_=pref_sb[:]), dma=True)
        state["T_PREF"] = t_pfr
        free("g2_sb", "prefflat", "prefp")

    if "E" in phases:
        b.barrier()
        h1_sb = state["h1_sb"]
        wg2_sb = state["wg2_sb"]
        glob_sb = sbuf("glob_sb", [128, NR, T], BF16)
        wvb = [sbuf("wvb0", [128, ND, 128], BF16), sbuf("wvb1", [128, ND, 128], BF16)]
        csb = [sbuf("csb0", [128, T], BF16), sbuf("csb1", [128, T], BF16)]
        v_sb = sbuf("v_sb", [128, T], F32)
        gE_sb = sbuf("gE_sb", [128, T], BF16)
        vg_sb = sbuf("vg_sb", [128, T], BF16)
        puE = [psum("puE0", [128, 512], F32), psum("puE1", [128, 512], F32)]
        pgE = [psum("pgE0", [128, 512], F32), psum("pgE1", [128, 512], F32)]

        t_wvld = [None] * NR
        t_csld = [None] * NR
        t_vmmN = [None, None]
        t_vevN = [None, None]
        t_gevN = [None, None]
        t_glob = [None] * NR
        for r in range(NR):
            pb = r % 2
            t_wvld[r] = b.task("gpsimd", [t_vmmN[pb]],
                               lambda e, r=r, pb=pb: e.dma_start(
                                   out=wvb[pb][:], in_=wv_t[:, r * 128:(r + 1) * 128]
                                   .rearrange("(c p) r -> p c r", p=128)), dma=True)
            t_csld[r] = b.task("sync", [t_glob[r - 2] if r >= 2 else None],
                               lambda e, r=r, pb=pb: e.dma_start(
                                   out=csb[pb][:], in_=cs_dram[r]), dma=True)
            t_vev = [None] * NT
            t_gev = [None] * NT
            for tc in range(NT):
                bank = tc % 2

                def em_vmm(e, r=r, pb=pb, tc=tc, bank=bank):
                    last = None
                    for d in range(ND):
                        last = e.matmul(puE[bank][:, :], wvb[pb][:, d, :],
                                        y_sb[:, d, 2 + tc * 512:2 + (tc + 1) * 512],
                                        start=(d == 0), stop=(d == ND - 1))
                    return last
                t_vmm = b.task("tensor", [t_wvld[r], t_vevN[bank],
                                          t_glob[r - 1] if r >= 1 and tc == 0 else None], em_vmm)
                t_vmmN[pb] = t_vmm
                t_vev[tc] = b.task("scalar", [t_vmm],
                                   lambda e, r=r, tc=tc, bank=bank: e.activation(
                                       out=v_sb[:, tc * 512:(tc + 1) * 512],
                                       in_=puE[bank][:, :], func=AF.Identity,
                                       bias=bv_sb[:, r:r + 1]))
                t_vevN[bank] = t_vev[tc]

                def em_gmm(e, r=r, tc=tc, bank=bank):
                    last = None
                    for hc in range(NH1):
                        last = e.matmul(pgE[bank][:, :], wg2_sb[:, hc, r * 128:(r + 1) * 128],
                                        h1_sb[:, hc, tc * 512:(tc + 1) * 512],
                                        start=(hc == 0), stop=(hc == NH1 - 1))
                    return last
                t_gmm = b.task("tensor", [t_gevN[bank],
                                          t_glob[r - 1] if r >= 1 and tc == 0 else None], em_gmm)
                t_gev[tc] = b.task("scalar", [t_gmm],
                                   lambda e, r=r, tc=tc, bank=bank: e.activation(
                                       out=gE_sb[:, tc * 512:(tc + 1) * 512],
                                       in_=pgE[bank][:, :], func=AF.Sigmoid,
                                       bias=bg2_sb[:, r:r + 1]))
                t_gevN[bank] = t_gev[tc]
            t_vg = b.task("vector", [t_vev[NT - 1], t_gev[NT - 1]],
                          lambda e: e.tensor_tensor(
                              out=vg_sb[:], in0=v_sb[:], in1=gE_sb[:], op=ALU.mult))
            t_glob[r] = b.task("vector", [t_csld[r], state["T_PREF"] if r == 0 else None],
                               lambda e, r=r, pb=pb: e.scalar_tensor_tensor(
                                   out=glob_sb[:, r, :], in0=csb[pb][:],
                                   scalar=pref_sb[:, r:r + 1], op0=ALU.add,
                                   in1=vg_sb[:], op1=ALU.mult))
            if "glob_dbg" in dbg:
                b.task("gpsimd", [t_glob[r]], lambda e, r=r: e.dma_start(
                    out=dbg["glob_dbg"][r], in_=glob_sb[:, r, :]), dma=True)
        state["T_GLOB"] = t_glob[NR - 1]
        state["glob_sb"] = glob_sb
        free("wvb0", "wvb1", "csb0", "csb1", "v_sb", "gE_sb", "vg_sb",
             "puE0", "puE1", "pgE0", "pgE1")

    if "F" in phases:
        b.barrier()
        free("totals_sb", "h1_sb", "wg2_sb", "y_sb")
        glob_sb = state["glob_sb"]
        m2_sb = sbuf("m2_sb", [1, T], F32, side="right")
        r2_sb = sbuf("r2_sb", [1, T], F32, side="right")
        a2_sb = sbuf("a2_sb", [1, T], F32, side="right")
        var2_sb = sbuf("var2_sb", [1, T], F32, side="right")
        tmpF_sb = sbuf("tmpF_sb", [1, T], F32, side="right")
        o1 = [sbuf("o1_0", [128, T], F32), sbuf("o1_1", [128, T], F32)]
        wuf = [sbuf("wuf0", [128, NR, 128], BF16), sbuf("wuf1", [128, NR, 128], BF16)]
        xlft = [sbuf("xlft0", [128, T], F32), sbuf("xlft1", [128, T], F32)]
        o1bf = sbuf("o1bf", [128, T], BF16)
        o1sq = sbuf("o1sq", [128, T], BF16)
        pF = [psum("pF0", [128, 512], F32), psum("pF1", [128, 512], F32)]
        stF = [psum(f"stF{i}", [64, 512], F32) for i in range(4)]

        t_wufld = [None] * ND
        t_xlfld = [None] * ND
        t_uvmmN = [None, None]
        t_o1evN = [None, None]
        t_stmm = [None] * ND
        t_o1st = [None] * ND
        t_scA = [None] * ND
        for d in range(ND):
            pb = d % 2
            t_wufld[d] = b.task("gpsimd", [t_uvmmN[pb]],
                                lambda e, d=d, pb=pb: e.dma_start(
                                    out=wuf[pb][:], in_=wu_t[:, d * 128:(d + 1) * 128]
                                    .rearrange("(c p) k -> p c k", p=128)), dma=True)
            t_xlfld[d] = b.task("sync", [t_o1st[d - 2] if d >= 2 else None],
                                lambda e, d=d, pb=pb: e.dma_start(
                                    out=xlft[pb][:], in_=xlf_dram[d]), dma=True)
            t_ev = [None] * NT
            for tc in range(NT):
                bank = tc % 2

                def em_uvmm(e, d=d, pb=pb, tc=tc, bank=bank):
                    last = None
                    for r in range(NR):
                        last = e.matmul(pF[bank][:, :], wuf[pb][:, r, :],
                                        glob_sb[:, r, tc * 512:(tc + 1) * 512],
                                        start=(r == 0), stop=(r == NR - 1))
                    return last
                t_uvmm = b.task("tensor", [t_wufld[d], t_o1evN[bank],
                                           state["T_GLOB"] if d == 0 else None], em_uvmm)
                t_uvmmN[pb] = t_uvmm
                t_ev[tc] = b.task("vector", [t_uvmm, t_xlfld[d],
                                             t_o1st[d - 2] if tc == 0 and d >= 2 else None,
                                             t_scA[d - 2] if tc == 0 and d >= 2 else None],
                                  lambda e, d=d, pb=pb, tc=tc, bank=bank: e.tensor_tensor(
                                      out=o1[pb][:, tc * 512:(tc + 1) * 512],
                                      in0=pF[bank][:, :],
                                      in1=xlft[pb][:, tc * 512:(tc + 1) * 512], op=ALU.add))
                t_o1evN[bank] = t_ev[tc]
            t_o1st[d] = b.task("sync", [t_ev[NT - 1]],
                               lambda e, d=d, pb=pb: e.dma_start(
                                   out=out1_dram[d], in_=o1[pb][:]), dma=True)
            t_sc = b.task("scalar", [t_ev[NT - 1], t_stmm[d - 1] if d >= 1 else None],
                          lambda e, d=d, pb=pb: [
                              e.copy(out=o1bf[:], in_=o1[pb][:]),
                              e.activation(out=o1sq[:], in_=o1[pb][:], func=AF.Square)][-1])
            t_scA[d] = t_sc

            def em_stmm(e, d=d):
                last = None
                for i in range(4):
                    e.matmul(stF[i][0:1, :], onesc_bf[:, :], o1bf[:, i * 512:(i + 1) * 512],
                             start=(d == 0), stop=(d == ND - 1))
                    last = e.matmul(stF[i][32:33, :], onesc_bf[:, :], o1sq[:, i * 512:(i + 1) * 512],
                                    start=(d == 0), stop=(d == ND - 1))
                return last
            t_stmm[d] = b.task("tensor", [t_sc], em_stmm)

        def em_fstat_v1(e):
            for i in range(4):
                e.tensor_scalar(out=m2_sb[0:1, i * 512:(i + 1) * 512], in0=stF[i][0:1, :],
                                scalar1=1.0 / D, scalar2=None, op0=ALU.mult)
                e.tensor_scalar(out=tmpF_sb[0:1, i * 512:(i + 1) * 512], in0=stF[i][32:33, :],
                                scalar1=1.0 / D, scalar2=None, op0=ALU.mult)
            e.tensor_tensor(out=var2_sb[0:1, :], in0=m2_sb[0:1, :], in1=m2_sb[0:1, :], op=ALU.mult)
            e.tensor_tensor(out=var2_sb[0:1, :], in0=tmpF_sb[0:1, :], in1=var2_sb[0:1, :],
                            op=ALU.subtract)
            return e.tensor_scalar(out=tmpF_sb[0:1, :], in0=var2_sb[0:1, :],
                                   scalar1=1e-5, scalar2=None, op0=ALU.add)
        t_fs1 = b.task("vector", [t_stmm[ND - 1]], em_fstat_v1)
        t_fs2 = b.task("scalar", [t_fs1], lambda e: e.activation(
            out=tmpF_sb[0:1, :], in_=tmpF_sb[0:1, :], func=AF.Sqrt))
        t_fs3 = b.task("vector", [t_fs2], lambda e: e.reciprocal(
            out=r2_sb[0:1, :], in_=tmpF_sb[0:1, :]))

        def em_fstat_v2(e):
            e.tensor_tensor(out=tmpF_sb[0:1, :], in0=var2_sb[0:1, :], in1=r2_sb[0:1, :], op=ALU.mult)
            e.tensor_tensor(out=tmpF_sb[0:1, :], in0=tmpF_sb[0:1, :], in1=r2_sb[0:1, :], op=ALU.mult)
            return e.tensor_scalar(out=tmpF_sb[0:1, :], in0=tmpF_sb[0:1, :],
                                   scalar1=1e-6, scalar2=None, op0=ALU.add)
        t_fs4 = b.task("vector", [t_fs3], em_fstat_v2)
        t_fs5 = b.task("scalar", [t_fs4], lambda e: e.activation(
            out=tmpF_sb[0:1, :], in_=tmpF_sb[0:1, :], func=AF.Sqrt))
        t_fs6 = b.task("vector", [t_fs5], lambda e: e.reciprocal(
            out=tmpF_sb[0:1, :], in_=tmpF_sb[0:1, :]))
        t_fs7 = b.task("vector", [t_fs6], lambda e: e.tensor_tensor(
            out=a2_sb[0:1, :], in0=r2_sb[0:1, :], in1=tmpF_sb[0:1, :], op=ALU.mult))
        state["T_STATS2"] = t_fs7
        state["m2_sb"] = m2_sb
        state["r2_sb"] = r2_sb
        if "stats_dbg" in dbg:
            def em_stdbg(e, sem):
                e.dma_start(out=dbg["stats_dbg"][0:1, :], in_=m2_sb[0:1, :]).then_inc(sem, 16)
                e.dma_start(out=dbg["stats_dbg"][1:2, :], in_=var2_sb[0:1, :]).then_inc(sem, 16)
                e.dma_start(out=dbg["stats_dbg"][2:3, :], in_=r2_sb[0:1, :]).then_inc(sem, 16)
                e.dma_start(out=dbg["stats_dbg"][3:4, :], in_=a2_sb[0:1, :]).then_inc(sem, 16)
            b.task("gpsimd", [t_fs7], em_stdbg, dma=True, inc=64, multi=True)

        b.barrier()
        free("wuf0", "wuf1", "xlft0", "xlft1", "o1bf", "o1sq",
             "pF0", "pF1", "stF0", "stF1", "stF2", "stF3")
        mbc2 = sbuf("mbc2", [128, T], F32)
        abc2 = sbuf("abc2", [128, T], F32)
        t1F = sbuf("t1F", [128, T], F32)
        xln1t = sbuf("xln1t", [128, T], BF16)
        ht = sbuf("ht", [128, T], BF16)
        bcp2 = psum("bcp2", [128, 512], F32)

        t_bc_mm = None
        t_bc_ev = None
        for j, (src, dst) in enumerate([(m2_sb, mbc2), (a2_sb, abc2)]):
            for i in range(4):
                t_bc_mm = b.task("tensor", [t_bc_ev],
                                 lambda e, src=src, i=i: e.matmul(
                                     bcp2[:, :], ones_f32[0:1, :], src[0:1, i * 512:(i + 1) * 512],
                                     start=True, stop=True))
                t_bc_ev = b.task("vector", [t_bc_mm],
                                 lambda e, dst=dst, i=i: e.tensor_copy(
                                     out=dst[:, i * 512:(i + 1) * 512], in_=bcp2[:, :]))
        T_BC2 = t_bc_ev

        t_nst = [None] * ND
        t_o1ld = [None] * ND
        t_n1 = [None] * ND
        t_ndbg = [None] * ND
        for d in range(ND):
            pb = d % 2
            t_o1ld[d] = b.task("sync", [t_n1[d - 2] if d >= 2 else None, t_o1st[d]],
                               lambda e, d=d, pb=pb: e.dma_start(
                                   out=o1[pb][:], in_=out1_dram[d]), dma=True)

            def em_n1(e, d=d, pb=pb):
                e.tensor_tensor(out=t1F[:], in0=o1[pb][:], in1=mbc2[:], op=ALU.subtract)
                e.scalar_tensor_tensor(out=xln1t[:], in0=t1F[:], scalar=ln1w_sb[:, d:d + 1],
                                       op0=ALU.mult, in1=abc2[:], op1=ALU.mult)
                return e.scalar_tensor_tensor(out=ht[:], in0=t1F[:], scalar=ln2w_sb[:, d:d + 1],
                                              op0=ALU.mult, in1=abc2[:], op1=ALU.mult)
            t_n1[d] = b.task("vector", [t_o1ld[d], T_BC2 if d < 2 else None,
                                        t_nst[d - 1] if d >= 1 else None,
                                        t_ndbg[d - 1] if d >= 1 else None], em_n1)

            def em_nst(e, sem, d=d):
                e.dma_start(out=ag1_in[d * 128:(d + 1) * 128, :], in_=xln1t[:]).then_inc(sem, 16)
                e.dma_start(out=h_dram[d], in_=ht[:]).then_inc(sem, 16)
            t_nst[d] = b.task("gpsimd", [t_n1[d]], em_nst, dma=True, inc=32, multi=True)
            if "xln1_dbg" in dbg:
                t_ndbg[d] = b.task("gpsimd", [t_n1[d]], lambda e, d=d: e.dma_start(
                    out=dbg["xln1_dbg"][d], in_=xln1t[:]), dma=True)
        state["T_N1"] = t_nst[ND - 1]
        state["T_N1_ALL"] = list(t_nst)
        free("t1F", "xln1t", "ht", "bcp2", "mbc2", "abc2", "o1_0", "o1_1")
        free("glob_sb")
        free("tmpF_sb", "var2_sb")

    if "M" in phases:
        b.barrier()
        msb = sbuf("msb", [128, NHEAD, 16], BF16)
        a_bf = sbuf("a_bf", [128, ND, 16], BF16)
        wo_sb = sbuf("wo_sb", [128, ND, 512], F32)
        a_f32 = sbuf("a_f32", [128, ND, 16], F32)
        pM = psum("pM", [128, 512], F32)

        def em_mld(e, sem):
            e.dma_start(out=wo_sb[:], in_=wo_sl.rearrange("(c p) h -> p c h", p=128)).then_inc(sem, 16)
            e.dma_start(out=a_f32[:], in_=a_in.rearrange("(c p) j -> p c j", p=128)).then_inc(sem, 16)
            e.dma_start(out=a_bf[:], in_=a_in.rearrange("(c p) j -> p c j", p=128)).then_inc(sem, 16)
        t_mld = b.task("gpsimd", [], em_mld, dma=True, inc=48, multi=True)
        t_mev = None
        for ct in range(NHEAD):
            def em_mmm(e, ct=ct):
                last = None
                for d in range(ND):
                    last = e.matmul(pM[:, 0:16], wo_sb[:, d, ct * 128:(ct + 1) * 128],
                                    a_f32[:, d, :], start=(d == 0), stop=(d == ND - 1))
                return last
            t_mmm = b.task("tensor", [t_mld, t_mev], em_mmm)
            t_mev = b.task("scalar", [t_mmm], lambda e, ct=ct: e.copy(
                out=msb[:, ct, :], in_=pM[:, 0:16]))
        state["T_M"] = t_mev
        state["msb"] = msb
        state["a_bf"] = a_bf
        if "msb_dbg" in dbg:
            b.task("gpsimd", [t_mev], lambda e: e.dma_start(
                out=dbg["msb_dbg"], in_=msb[:]), dma=True)
        free("wo_sb", "a_f32", "pM")

    if "H" in phases:
        b.barrier()

        t_ag1 = b.task("gpsimd", state["T_N1_ALL"], lambda e: e.collective_compute(
            "AllGather", ALU.bypass, replica_groups=GROUPS,
            ins=[ag1_in.opt()], outs=[ag1_out.opt()]), use_cc=True)
        wq_sb = [sbuf(f"wq_sb{i}", [128, ND, 128], BF16) for i in range(2)]
        wk_sb = [sbuf(f"wk_sb{i}", [128, ND, 128], BF16) for i in range(2)]
        wv2_sb = [sbuf(f"wv2_sb{i}", [128, ND, 128], BF16) for i in range(2)]
        Q_sb = sbuf("Q_sb", [128, 16, 512], BF16)
        K_sb = sbuf("K_sb", [128, 16, 512], BF16)
        vaug = sbuf("vaug", [128, 64, 129], BF16)
        Gt = [sbuf("Gt0", [128, ND, 512], BF16), sbuf("Gt1", [128, ND, 512], BF16)]
        vtmp = sbuf("vtmp", [128, 512], BF16)
        qsq = sbuf("qsq", [128, 512], BF16)
        PTB = [sbuf(f"PTB{i}", [128, 1024], BF16) for i in range(3)]
        PTrawB = sbuf("PTrawB", [128, 1024], BF16)
        O_sb = [sbuf("O_sb0", [128, 64, 128], BF16), sbuf("O_sb1", [128, 64, 128], BF16)]
        otb = sbuf("otb", [128, 128], BF16)
        rc_sb = sbuf("rc_sb", [128, 1], F32)
        runq = sbuf("runq", [1, 1], F32)
        runk = sbuf("runk", [1, 1], F32)
        nqt = sbuf("nqt", [1, 1], F32)
        bias_h = sbuf("bias_h", [128, 1], F32)
        modev = sbuf("modev", [16, 512], F32)

        def mk_wqkv(h, deps):
            def em_wqkv(e, sem, h=h):
                hp = h % 2
                for dst, src in ((wq_sb, wq_t), (wk_sb, wk_t), (wv2_sb, wv2_t)):
                    e.dma_start(out=dst[hp][:],
                                in_=src[:, h * 128:(h + 1) * 128]
                                .rearrange("(c p) k -> p c k", p=128)).then_inc(sem, 16)
            return b.task("gpsimd", deps, em_wqkv, dma=True, inc=48, multi=True)
        t_wqkv_h = [mk_wqkv(0, [])]

        t_vone = b.task("gpsimd", [], lambda e: e.memset(vaug[:, :, 128:129], 1.0))

        T_MOD_H = []
        t_gld = [None] * 16
        t_chunk_done = None
        t_postpass_done = None
        t_mevG = None
        t_mstG = None
        epi_head = {}

        def post_st(hh, st, ptOp, pmodp, first_dep):
            nonlocal t_mevG, t_mstG
            t_ote = None
            t_mmm = None
            for i in range(4):
                t_otr = b.task("tensor", [epi_head[(hh, st)], first_dep if i == 0 else None,
                                          t_ote],
                               lambda e, hh=hh, st=st, i=i: e.transpose(
                                   ptOp[:, :], O_sb[hh % 2][:, st * 4 + i, :], ident_sb[:, :]))
                t_ote = b.task("vector", [t_otr], lambda e: e.tensor_copy(
                    out=otb[:, :], in_=ptO_dummy_read(ptOp)))
                t_mmm = b.task("tensor", [t_ote, t_mevG if i == 0 else None],
                               lambda e, hh=hh, i=i: e.matmul(
                                   pmodp[:, i * 128:(i + 1) * 128], msb[:, hh, :], otb[:, :],
                                   start=True, stop=True))
            t_mevG = b.task("vector", [t_mmm, t_mstG], lambda e, pmodp=pmodp: e.tensor_copy(
                out=modev[:, :], in_=pmodp[:, :]))
            t_mstG = b.task("gpsimd", [t_mevG], lambda e, hh=hh, st=st: e.dma_start(
                out=modh_dram[hh][:, st * 512:(st + 1) * 512], in_=modev[:, :]), dma=True)
            T_MOD_H.append(t_mstG)

        def ptO_dummy_read(ptOp):
            return ptOp[:, :]

        for h in range(NHEAD):
            pq = psum(f"pq{h}", [128, 512], F32)
            pk = psum(f"pk{h}", [128, 512], F32)
            pv = psum(f"pv{h}", [128, 512], F32)
            ptr2 = [psum(f"ptr{h}a", [128, 128], BF16), psum(f"ptr{h}b", [128, 128], BF16)]
            pnrm = psum(f"pnrm{h}", [64, 512], F32)
            ptOp = psum(f"ptOp{h}", [128, 128], BF16)
            pmodp = psum(f"pmodp{h}", [16, 512], F32)
            t_qev = [None] * 16
            t_kev = [None] * 16
            t_vtr = [None] * 16
            t_nrm = [None] * 16
            for s in range(16):
                pb = s % 2
                c, cc = s // 4, s % 4
                t_gld[s] = b.task("sync",
                                  [t_ag1 if h == 0 else None,
                                   t_chunk_done if h > 0 and s < 2 else None,
                                   t_qev[s - 2] if s >= 2 else None,
                                   t_kev[s - 2] if s >= 2 else None,
                                   t_vtr[s - 2] if s >= 2 else None],
                                  lambda e, s=s, pb=pb, c=c, cc=cc: e.dma_start(
                                      out=Gt[pb][:],
                                      in_=ag1_out[c * D:(c + 1) * D, cc * 512:(cc + 1) * 512]
                                      .rearrange("(d p) t -> p d t", p=128)), dma=True)

                def em_proj(e, h=h, s=s, pb=pb):
                    hp = h % 2
                    for d in range(ND):
                        e.matmul(pq[:, :], wq_sb[hp][:, d, :],
                                 Gt[pb][:, d, :], start=(d == 0), stop=(d == ND - 1))
                    for d in range(ND):
                        e.matmul(pk[:, :], wk_sb[hp][:, d, :],
                                 Gt[pb][:, d, :], start=(d == 0), stop=(d == ND - 1))
                    last = None
                    for d in range(ND):
                        last = e.matmul(pv[:, :], wv2_sb[hp][:, d, :],
                                        Gt[pb][:, d, :], start=(d == 0), stop=(d == ND - 1))
                    return last
                t_proj = b.task("tensor", [t_gld[s], t_wqkv_h[h],
                                           t_qev[s - 1] if s >= 1 else None,
                                           t_kev[s - 1] if s >= 1 else None,
                                           t_vev_last if s >= 1 else None,
                                           t_chunk_done if s == 0 and h > 0 else None], em_proj)
                t_qev[s] = b.task("scalar", [t_proj], lambda e, s=s: e.copy(
                    out=Q_sb[:, s, :], in_=pq[:, :]))
                t_kev[s] = b.task("scalar", [t_proj], lambda e, s=s: e.copy(
                    out=K_sb[:, s, :], in_=pk[:, :]))
                t_vev = b.task("vector", [t_proj], lambda e: e.tensor_copy(
                    out=vtmp[:], in_=pv[:, :]))
                t_vev_last = t_vev
                t_tr2 = [None, None]
                for c4 in range(4):
                    t_trm = b.task("tensor", [t_vev, t_tr2[c4 % 2]],
                                   lambda e, c4=c4: e.transpose(
                                       ptr2[c4 % 2][:, :], vtmp[:, c4 * 128:(c4 + 1) * 128],
                                       ident_sb[:, :]))
                    t_tr2[c4 % 2] = b.task("vector", [t_trm], lambda e, s=s, c4=c4: e.tensor_copy(
                        out=vaug[:, s * 4 + c4, 0:128], in_=ptr2[c4 % 2][:, :]))
                t_vtr[s] = t_tr2[1]
                t_sqq = b.task("scalar", [t_qev[s], t_nrm[s - 1] if s >= 1 else None],
                               lambda e, s=s: e.activation(out=qsq[:], in_=Q_sb[:, s, :],
                                                           func=AF.Square))
                t_nq = b.task("tensor", [t_sqq], lambda e: e.matmul(
                    pnrm[0:1, :], onesc_bf[:, :], qsq[:], start=True, stop=True))
                t_rq = b.task("vector", [t_nq], lambda e, s=s: [
                    e.reduce_max(nqt[0:1, 0:1], pnrm[0:1, :], axis=mybir.AxisListType.X),
                    (e.tensor_tensor(out=runq[0:1, :], in0=runq[0:1, :], in1=nqt[0:1, :], op=ALU.max)
                     if s > 0 else e.tensor_copy(out=runq[0:1, :], in_=nqt[0:1, :]))][-1])
                t_sqk = b.task("scalar", [t_kev[s], t_rq],
                               lambda e, s=s: e.activation(out=qsq[:], in_=K_sb[:, s, :],
                                                           func=AF.Square))
                t_nk = b.task("tensor", [t_sqk], lambda e: e.matmul(
                    pnrm[0:1, :], onesc_bf[:, :], qsq[:], start=True, stop=True))
                t_nrm[s] = b.task("vector", [t_nk], lambda e, s=s: [
                    e.reduce_max(nqt[0:1, 0:1], pnrm[0:1, :], axis=mybir.AxisListType.X),
                    (e.tensor_tensor(out=runk[0:1, :], in0=runk[0:1, :], in1=nqt[0:1, :], op=ALU.max)
                     if s > 0 else e.tensor_copy(out=runk[0:1, :], in_=nqt[0:1, :]))][-1])
                if h >= 1:
                    post_st(h - 1, s, ptOp, pmodp,
                            t_chunk_done if s == 0 else None)
            if h + 1 < NHEAD:
                t_wqkv_h.append(mk_wqkv(h + 1, [t_proj]))
            t_b1 = b.task("vector", [t_nrm[15]], lambda e: e.tensor_tensor(
                out=nqt[0:1, :], in0=runq[0:1, :], in1=runk[0:1, :], op=ALU.mult))
            t_b2 = b.task("scalar", [t_b1], lambda e: e.activation(
                out=nqt[0:1, :], in_=nqt[0:1, :], func=AF.Sqrt, scale=1.0 / 128.0))
            t_b3 = b.task("vector", [t_b2], lambda e: e.tensor_scalar(
                out=nqt[0:1, :], in0=nqt[0:1, :], scalar1=-1.0, scalar2=None, op0=ALU.mult))
            t_b4 = b.task("tensor", [t_b3], lambda e: e.matmul(
                pv[:, 0:1], ones_f32[0:1, :], nqt[0:1, 0:1], start=True, stop=True))
            t_b5 = b.task("vector", [t_b4], lambda e: e.tensor_copy(
                out=bias_h[:, :], in_=pv[:, 0:1]))
            if "bh_dbg" in dbg:
                b.task("gpsimd", [t_b5], lambda e, h=h: e.dma_start(
                    out=dbg["bh_dbg"][h:h + 1, :], in_=nqt[0:1, :]), dma=True)
            free(f"pmodp{h}", f"ptOp{h}", f"pnrm{h}", f"ptr{h}b", f"ptr{h}a", f"pv{h}", f"pk{h}", f"pq{h}")

            psS = [psum(f"psSA{h}", [128, 1024], F32), psum(f"psSB{h}", [128, 1024], F32)]
            psO2 = [psum(f"psO{j}_{h}", [128, 512], F32) for j in range(4)]

            def obank(st, i):
                return psO2[2 * (st % 2) + i // 2][:, (i % 2) * 256:(i % 2) * 256 + 129]

            t_pv = {}
            t_exp = {}
            t_epi_done = {}

            def make_pair(st, j):
                pb = j % 2
                ptb = j % 3

                def em_spair(e, h=h, st=st, j=j, pb=pb):
                    last = None
                    for half in range(2):
                        kc = 2 * j + half
                        last = e.matmul(psS[pb][:, half * 512:(half + 1) * 512],
                                        K_sb[:, kc // 4, (kc % 4) * 128:(kc % 4 + 1) * 128],
                                        Q_sb[:, st, :], start=True, stop=True)
                    return last
                t_smm = b.task("tensor", [(t_b5 if (st == 0 and j == 0) else None),
                                          (t_postpass_done if (st == 0 and j == 0) else None),
                                          t_exp.get((st, j - 2)),
                                          t_exp.get((st - 1, 2 * st - 2 + j))
                                          if st >= 1 and j < 2 else None], em_spair)
                diag = j >= 2 * st
                prev_pv = t_pv.get((st, j - 3))
                if prev_pv is None and st >= 1 and j < 3:
                    cand = [jj for jj in range(2 * st) if jj % 3 == j % 3]
                    if cand:
                        prev_pv = t_pv.get((st - 1, cand[-1]))
                if diag:
                    m = j - 2 * st

                    def em_exp(e, pb=pb):
                        return e.activation(out=PTrawB[:], in_=psS[pb][:, :], func=AF.Exp,
                                            scale=ISQH, bias=bias_h[:, :])
                    t_expr = b.task("scalar", [t_smm,
                                               t_exp.get((st, j - 1)) if j - 1 >= 2 * st
                                               else None], em_exp)
                    t_exp[(st, j)] = b.task("vector", [t_expr, prev_pv],
                                            lambda e, ptb=ptb, m=m: e.tensor_tensor(
                                                out=PTB[ptb][:], in0=PTrawB[:],
                                                in1=masks2_sb[:, m, :], op=ALU.mult))
                else:
                    def em_exp2(e, pb=pb, ptb=ptb):
                        return e.activation(out=PTB[ptb][:], in_=psS[pb][:, :], func=AF.Exp,
                                            scale=ISQH, bias=bias_h[:, :])
                    t_exp[(st, j)] = b.task("scalar", [t_smm, prev_pv], em_exp2)

            def make_pvpair(st, j):
                npair = 2 * (st + 1)
                ptb = j % 3

                def em_pv(e, st=st, j=j, ptb=ptb, npair=npair):
                    last = None
                    for half in range(2):
                        kc = 2 * j + half
                        for i in range(4):
                            last = e.matmul(obank(st, i),
                                            PTB[ptb][:, half * 512 + i * 128:half * 512 + (i + 1) * 128],
                                            vaug[:, kc, :], start=(kc == 0 and i % 2 == 0),
                                            stop=(j == npair - 1 and half == 1),
                                            skip_group_check=True)
                    return last
                t_pv[(st, j)] = b.task("tensor", [t_exp[(st, j)],
                                                  t_epi_done.get(st - 2) if j == 0 else None,
                                                  t_vone if h == 0 and st == 0 and j == 0
                                                  else None], em_pv)

            def make_epilogue(st):
                npair = 2 * (st + 1)
                t_on = None
                for i in range(4):
                    t_rc = b.task("vector", [t_pv[(st, npair - 1)]],
                                  lambda e, st=st, i=i: e.reciprocal(
                                      out=rc_sb[:, :], in_=obank(st, i)[:, 128:129]))
                    t_on = b.task("vector", [t_rc], lambda e, h=h, st=st, i=i: e.tensor_scalar(
                        out=O_sb[h % 2][:, st * 4 + i, :], in0=obank(st, i)[:, 0:128],
                        scalar1=rc_sb[:, :], scalar2=None, op0=ALU.mult))
                t_epi_done[st] = t_on
                epi_head[(h, st)] = t_on

            for st in range(16):
                npair = 2 * (st + 1)
                make_pair(st, 0)
                for j in range(npair):
                    if j + 1 < npair:
                        make_pair(st, j + 1)
                    make_pvpair(st, j)
                make_epilogue(st)
            t_chunk_done = t_epi_done[15]
            t_postpass_done = t_mevG
            free(f"psO3_{h}", f"psO2_{h}", f"psO1_{h}", f"psO0_{h}", f"psSB{h}", f"psSA{h}")

        ptOp = psum("ptOp_t", [128, 128], BF16)
        pmodp = psum("pmodp_t", [16, 512], F32)
        for st in range(16):
            post_st(NHEAD - 1, st, ptOp, pmodp, t_chunk_done if st == 0 else None)
        free("pmodp_t", "ptOp_t")
        T_ATTN = T_MOD_H[-1]
        free("modev", "bias_h", "nqt", "runk", "runq", "rc_sb", "otb",
             "PTrawB", "PTB2", "PTB1", "PTB0", "qsq", "vtmp", "Gt1", "Gt0", "vaug",
             "K_sb", "Q_sb", "wv2_sb0", "wv2_sb1", "wk_sb0", "wk_sb1",
             "wq_sb0", "wq_sb1", "O_sb1", "O_sb0")
        state["T_ATTN"] = T_ATTN

    if "I" in phases:
        b.barrier()
        mh_sb = sbuf("mh_sb", [64, 512], F32)
        modev2 = sbuf("modev2", [16, 512], F32)
        modt = sbuf("modt", [16, T], BF16, side="right")
        modt_in = sbuf("modt_in", [16, T], F32)
        pI = psum("pI", [16, 512], F32)
        t_prev_ev = None
        t_ld_prev = None
        t_st_all = []
        for tcS in range(16):
            cg, tl = tcS // 4, tcS % 4
            t_mld = b.task("sync", [t_prev_ev],
                           lambda e, tcS=tcS: e.dma_start(
                               out=mh_sb[:], in_=modh_dram[:, :, tcS * 512:(tcS + 1) * 512]
                               .rearrange("h j t -> (h j) t")), dma=True)
            t_msum = b.task("tensor", [t_mld, t_prev_ev], lambda e: e.matmul(
                pI[:, :], selh_sb[:, :], mh_sb[:], start=True, stop=True))
            t_prev_ev = b.task("vector", [t_msum, t_ld_prev], lambda e: e.tensor_copy(
                out=modev2[:, :], in_=pI[:, :]))
            t_st = b.task("sync", [t_prev_ev], lambda e, cg=cg, tl=tl: e.dma_start(
                out=rs3_in.rearrange("(c j) t -> j c t", j=16)[:, cg, tl * 512:(tl + 1) * 512],
                in_=modev2[:, :]), dma=True)
            if "mod_dbg" in dbg:
                b.task("gpsimd", [t_prev_ev], lambda e, tcS=tcS: e.dma_start(
                    out=dbg["mod_dbg"][:, tcS * 512:(tcS + 1) * 512], in_=modev2[:, :]), dma=True)
            t_ld_prev = t_st
            t_st_all.append(t_st)
        t_rs3 = b.task("gpsimd", t_st_all, lambda e: e.collective_compute(
            "ReduceScatter", ALU.add, replica_groups=GROUPS,
            ins=[rs3_in.opt()], outs=[rs3_out.opt()]), use_cc=True)
        t_mtl = b.task("sync", [t_rs3], lambda e: e.dma_start(
            out=modt_in[:], in_=rs3_out), dma=True)
        t_tanh = b.task("scalar", [t_mtl], lambda e: e.activation(
            out=modt[:, :], in_=modt_in[:, :], func=AF.Tanh))
        if "modf_dbg" in dbg:
            b.task("gpsimd", [t_tanh], lambda e: e.dma_start(
                out=dbg["modf_dbg"], in_=modt[:, :]), dma=True)
        state["T_MODT"] = t_tanh
        state["modt"] = modt
        free("pI", "modt_in", "modev2", "mh_sb")

    if "J" in phases:
        b.barrier()
        modt = state["modt"]
        h_sb = sbuf("h_sb", [128, ND, T], BF16)
        hA_sb = sbuf("hA_sb", [16, T], F32)
        modA = sbuf("modA", [16, T], BF16)
        b_sb = sbuf("b_sb", [16, FF], BF16)
        winb = [sbuf("winb0", [128, ND, 128], BF16), sbuf("winb1", [128, ND, 128], BF16)]
        wupb = [sbuf("wupb0", [128, ND, 128], BF16), sbuf("wupb1", [128, ND, 128], BF16)]
        gateb = sbuf("gateb", [128, NT, 512], BF16)
        upb = sbuf("upb", [128, NT, 512], BF16)
        gub = [sbuf("gub0", [128, T], BF16), sbuf("gub1", [128, T], BF16)]
        pgJ = [psum(f"pgJ{i}", [128, 512], F32) for i in range(4)]
        pqJ = [psum(f"pqJ{i}", [128, 512], F32) for i in range(4)]

        t_hld = b.task("sync", [], lambda e: e.dma_start(
            out=h_sb[:], in_=h_dram.rearrange("d p t -> p d t")), dma=True)
        t_bld = b.task("gpsimd", [], lambda e: e.dma_start(out=b_sb[:], in_=b_in), dma=True)
        t_haev = None
        for tc in range(NT):
            def em_ha(e, tc=tc):
                last = None
                for d in range(ND):
                    last = e.matmul(pgJ[0][0:16, :], state["a_bf"][:, d, :],
                                    h_sb[:, d, tc * 512:(tc + 1) * 512],
                                    start=(d == 0), stop=(d == ND - 1))
                return last
            t_ha = b.task("tensor", [t_hld, state["T_M"], t_haev], em_ha)
            t_haev = b.task("vector", [t_ha], lambda e, tc=tc: e.tensor_copy(
                out=hA_sb[:, tc * 512:(tc + 1) * 512], in_=pgJ[0][0:16, :]))
        t_modA = b.task("vector", [t_haev, state["T_MODT"]], lambda e: e.scalar_tensor_tensor(
            out=modA[:, :], in0=hA_sb[:, :], scalar=0.1, op0=ALU.mult,
            in1=modt[:, :], op1=ALU.mult))

        t_wld = [None] * NF
        t_guv4 = [None] * NT
        t_mmlast = [None, None]
        t_gev4 = [None] * NT
        t_uev4 = [None] * NT
        t_gust = [None] * NF
        for f in range(NF):
            pb = f % 2

            def em_wld(e, sem, f=f, pb=pb):
                e.dma_start(out=winb[pb][:], in_=win_t[:, f * 128:(f + 1) * 128]
                            .rearrange("(c p) k -> p c k", p=128)).then_inc(sem, 16)
                e.dma_start(out=wupb[pb][:], in_=wup_t[:, f * 128:(f + 1) * 128]
                            .rearrange("(c p) k -> p c k", p=128)).then_inc(sem, 16)
            t_wld[f] = b.task("gpsimd", [t_mmlast[pb]], em_wld, dma=True, inc=32, multi=True)

            def em_gmm(e, f=f, pb=pb):
                for d in range(ND):
                    for tc in range(NT):
                        e.matmul(pgJ[tc][:, :], winb[pb][:, d, :],
                                 h_sb[:, d, tc * 512:(tc + 1) * 512],
                                 start=(d == 0), stop=False)
                last = None
                for tc in range(NT):
                    last = e.matmul(pgJ[tc][:, :], b_sb[0:16, f * 128:(f + 1) * 128],
                                    modA[:, tc * 512:(tc + 1) * 512], start=False, stop=True)
                return last
            t_gmm = b.task("tensor", [t_wld[f], t_modA if f == 0 else None] +
                           t_gev4, em_gmm)

            def em_umm(e, f=f, pb=pb):
                last = None
                for d in range(ND):
                    for tc in range(NT):
                        last = e.matmul(pqJ[tc][:, :], wupb[pb][:, d, :],
                                        h_sb[:, d, tc * 512:(tc + 1) * 512],
                                        start=(d == 0), stop=(d == ND - 1))
                return last
            t_umm = b.task("tensor", t_uev4 + [], em_umm)
            t_mmlast[pb] = t_umm
            t_guv = None
            for tc in range(NT):
                t_gev4[tc] = b.task("scalar", [t_gmm, t_guv4[tc]], lambda e, tc=tc: e.activation(
                    out=gateb[:, tc, :], in_=pgJ[tc][:, :], func=AF.Silu))
                t_uev4[tc] = b.task("vector", [t_umm, t_guv4[tc]], lambda e, tc=tc: e.tensor_copy(
                    out=upb[:, tc, :], in_=pqJ[tc][:, :]))
                t_guv = b.task("vector", [t_gev4[tc], t_uev4[tc],
                                          t_gust[f - 2] if tc == 0 and f >= 2 else None],
                               lambda e, f=f, pb=pb, tc=tc: e.tensor_tensor(
                                   out=gub[pb][:, tc * 512:(tc + 1) * 512],
                                   in0=gateb[:, tc, :], in1=upb[:, tc, :], op=ALU.mult))
                t_guv4[tc] = t_guv
            t_gust[f] = b.task("sync", [t_guv], lambda e, f=f, pb=pb: e.dma_start(
                out=gu_dram[f], in_=gub[pb][:]), dma=True)
            if "ga_dbg" in dbg:
                b.task("gpsimd", [t_guv], lambda e, f=f, pb=pb: e.dma_start(
                    out=dbg["ga_dbg"][f], in_=gub[pb][:]), dma=True)
        state["T_GU"] = t_gust[NF - 1]
        free("pqJ3", "pqJ2", "pqJ1", "pqJ0", "pgJ3", "pgJ2", "pgJ1", "pgJ0", "gub1", "gub0", "upb", "gateb",
             "wupb1", "wupb0", "winb1", "winb0", "b_sb", "modA", "hA_sb", "h_sb")

    if "K" in phases:
        b.barrier()
        gu_sb = sbuf("gu_sb", [128, NF, 512], BF16)
        wdnb = [sbuf("wdnb0", [128, NF, 128], BF16), sbuf("wdnb1", [128, NF, 128], BF16)]
        o1k = [sbuf("o1k0", [128, 512], F32), sbuf("o1k1", [128, 512], F32)]
        mbK = sbuf("mbK", [128, 512], F32)
        rbK = sbuf("rbK", [128, 512], F32)
        t1K = sbuf("t1K", [128, 512], F32)
        finb = [sbuf("finb0", [128, 512], F32), sbuf("finb1", [128, 512], F32)]
        pK = [psum("pK0", [128, 512], F32), psum("pK1", [128, 512], F32)]
        pbK = psum("pbK", [128, 512], F32)

        t_fin = {}
        t_fst = {}
        t_guld = [None] * NT
        t_wdld = {}
        t_mmN = [None, None]
        t_bcev = None
        for tb in range(NT):
            t_guld[tb] = b.task("sync", [t_fst.get((tb - 1, ND - 1)), state["T_GU"] if tb == 0 else None],
                                lambda e, tb=tb: e.dma_start(
                                    out=gu_sb[:], in_=gu_dram[:, :, tb * 512:(tb + 1) * 512]
                                    .rearrange("f p t -> p f t")), dma=True)
            t_bm = b.task("tensor", [t_bcev, state["T_STATS2"] if tb == 0 else None,
                                     t_fin.get((tb - 1, ND - 1))],
                          lambda e, tb=tb: e.matmul(
                              pbK[:, :], ones_f32[0:1, :],
                              state["m2_sb"][0:1, tb * 512:(tb + 1) * 512], start=True, stop=True))
            t_bmev = b.task("vector", [t_bm], lambda e: e.tensor_copy(out=mbK[:], in_=pbK[:, :]))
            t_br = b.task("tensor", [t_bmev], lambda e, tb=tb: e.matmul(
                pbK[:, :], ones_f32[0:1, :],
                state["r2_sb"][0:1, tb * 512:(tb + 1) * 512], start=True, stop=True))
            t_bcev = b.task("vector", [t_br], lambda e: e.tensor_copy(out=rbK[:], in_=pbK[:, :]))
            for d in range(ND):
                pb = d % 2
                bank = d % 2
                t_wdld[(tb, d)] = b.task("gpsimd", [t_mmN[pb]],
                                         lambda e, tb=tb, d=d, pb=pb: e.dma_start(
                                             out=wdnb[pb][:],
                                             in_=wdnbf_dram[d].rearrange(
                                                 "p (f c) -> p f c", f=NF)), dma=True)
                t_o1l = b.task("sync", [t_fin.get((tb, d - 2))],
                               lambda e, tb=tb, d=d, pb=pb: e.dma_start(
                                   out=o1k[pb][:], in_=out1_dram[d][:, tb * 512:(tb + 1) * 512]),
                               dma=True)

                def em_kmm(e, tb=tb, d=d, pb=pb, bank=bank):
                    last = None
                    for f in range(NF):
                        last = e.matmul(pK[bank][:, :], wdnb[pb][:, f, :],
                                        gu_sb[:, f, :], start=(f == 0), stop=(f == NF - 1))
                    return last
                t_kmm = b.task("tensor", [t_wdld[(tb, d)], t_guld[tb],
                                          t_fin.get((tb, d - 2))], em_kmm)
                t_mmN[pb] = t_kmm

                def em_fin(e, tb=tb, d=d, pb=pb, bank=bank):
                    e.tensor_tensor(out=t1K[:], in0=o1k[pb][:], in1=mbK[:], op=ALU.subtract)
                    e.tensor_tensor(out=t1K[:], in0=t1K[:], in1=rbK[:], op=ALU.mult)
                    e.tensor_tensor(out=t1K[:], in0=t1K[:], in1=o1k[pb][:], op=ALU.add)
                    return e.tensor_tensor(out=finb[pb][:], in0=t1K[:], in1=pK[bank][:, :],
                                           op=ALU.add)
                t_fin[(tb, d)] = b.task("vector", [t_kmm, t_o1l, t_bcev,
                                                   t_fst.get((tb, d - 2))], em_fin)
                t_fst[(tb, d)] = b.task("sync", [t_fin[(tb, d)]],
                                        lambda e, tb=tb, d=d, pb=pb: e.dma_start(
                                            out=out_ext[d * 128:(d + 1) * 128,
                                                        tb * 512:(tb + 1) * 512],
                                            in_=finb[pb][:]), dma=True)
        b.barrier()

    state["dbg"] = dbg
    return nc, b, state



def _pcol(v):
    v = np.asarray(v)
    n = v.shape[0]
    return np.ascontiguousarray(v.reshape(n // 128, 128).T)


def stage_inputs(inp):
    x = np.asarray(inp["L0_attn_x"], dtype=np.float32)
    shared = {
        "wu_t": np.ascontiguousarray(np.asarray(inp["L0_attn_Wu"]).T),
        "wv_t": np.ascontiguousarray(np.asarray(inp["L0_attn_Wv"]).T),
        "wg1_t": np.ascontiguousarray(np.asarray(inp["L0_attn_Wg1"]).T),
        "wg2_t": np.ascontiguousarray(np.asarray(inp["L0_attn_Wg2"]).T),
        "bu": _pcol(inp["L0_attn_bu"]),
        "bv": _pcol(inp["L0_attn_bv"]),
        "bg1": _pcol(inp["L0_attn_bg1"]),
        "bg2": _pcol(inp["L0_attn_bg2"]),
        "convw": np.ascontiguousarray(
            np.asarray(inp["L0_attn_conv_w"])[:, 0, :].reshape(ND, 128, 3).transpose(1, 0, 2)),
        "convb": _pcol(inp["L0_attn_conv_b"]),
        "ln1w": _pcol(inp["L0_ffn_ln1_w"]),
        "ln2w": _pcol(inp["L0_ffn_ln2_w"]),
        "a_mat": np.ascontiguousarray(np.asarray(inp["L0_ffn_A"])),
        "b_mat": np.ascontiguousarray(np.asarray(inp["L0_ffn_B"])),
        "win_t": np.ascontiguousarray(np.asarray(inp["L0_ffn_W_in_base"]).T),
        "wup_t": np.ascontiguousarray(np.asarray(inp["L0_ffn_W_up_base"]).T),
        "wdn_t": np.ascontiguousarray(np.asarray(inp["L0_ffn_W_down"]).T),
    }
    wq = np.asarray(inp["L0_ffn_Wq"])
    wk = np.asarray(inp["L0_ffn_Wk"])
    wv2 = np.asarray(inp["L0_ffn_Wv"])
    wo = np.asarray(inp["L0_ffn_Wo"])
    kk = np.arange(128)[:, None]
    qq = np.arange(512)[None, :]
    masks = np.stack([(128 * j + kk <= qq).astype(np.float32) for j in range(4)], axis=1)
    shared["masks"] = np.ascontiguousarray(masks)
    m2p = np.stack([np.concatenate([masks[:, 2 * m, :], masks[:, 2 * m + 1, :]], axis=1)
                    for m in range(2)], axis=1)
    shared["masks2"] = np.ascontiguousarray(m2p)
    shared["ident"] = np.eye(128, dtype=np.float32)
    shared["selh"] = np.ascontiguousarray(np.tile(np.eye(16, dtype=np.float32), (4, 1)))

    in_maps = []
    for c in range(8):
        bidx, rank = c // 4, c % 4
        chunk = x[bidx, rank * T:(rank + 1) * T, :]
        x_ext = np.zeros((D, TX), dtype=np.float32)
        x_ext[:, 2:2 + T] = chunk.T
        if rank > 0:
            x_ext[:, 1] = x[bidx, rank * T - 1, :]
        if rank < 3:
            x_ext[:, 2 + T] = x[bidx, (rank + 1) * T, :]
        sel = np.zeros((4, 1), dtype=np.float32)
        sel[:rank] = 1.0
        hs = slice(512 * rank, 512 * (rank + 1))
        m = dict(shared)
        m.update({
            "x_ext": x_ext,
            "sel": sel,
            "wq_t": np.ascontiguousarray(wq.T[:, hs]),
            "wk_t": np.ascontiguousarray(wk.T[:, hs]),
            "wv2_t": np.ascontiguousarray(wv2.T[:, hs]),
            "wo_sl": np.ascontiguousarray(wo[:, hs]),
        })
        in_maps.append(m)
    return in_maps


def kernel(**inputs):
    in_maps = stage_inputs(inputs)
    nc, b, state = build_program()
    b.emit()
    res = run_bass_kernel_spmd(nc, in_maps, core_ids=list(range(8)))
    out = np.empty((2, S, D), dtype=np.float32)
    for c in range(8):
        bidx, rank = c // 4, c % 4
        out[bidx, rank * T:(rank + 1) * T, :] = res.results[c]["out"].T
    return out
